# version 05
# baseline (speedup 1.0000x reference)
import contextlib
import math
import numpy as np
import concourse.bass as bass
import concourse.mybir as mybir
from concourse.bass_utils import run_bass_kernel_spmd

F32 = mybir.dt.float32
BF16 = mybir.dt.bfloat16
AF = mybir.ActivationFunctionType
ALU = mybir.AluOpType
AX = mybir.AxisListType

P = 128
D = 1024
DH = 8
T = 8192
N = 16
TB = 90
NSEG = 9
NCH = 91
HID = 4096
HH = 32
T2 = 448
NCH2 = (T + T2 - 1) // T2

MAGIC = 12582912.0
SC_DT_SCALE = 0.5 / (0.0495 + 1e-8)
SC_DT_BIAS = -0.0505 * SC_DT_SCALE
SC_DT_RNG = 0.099
SC_DT_CTR = 0.0505
SC8_SCALE = 0.5 / (8.0 + 1e-8)
SC5_SCALE = 0.5 / (5.0 + 1e-8)


class Sched:
    def __init__(self):
        self.ops = {e: [] for e in ("v", "a", "t", "p", "s")}
        self.cnt = {e: 0 for e in ("v", "a", "t", "p", "q")}

    def rec(self, eng, fn, waits=(), inc=None):
        self.ops[eng].append((fn, list(waits), inc))
        if inc is not None:
            self.cnt[inc] += 16 if inc == "q" else 1

    def w(self, *engs):
        return [(e, self.cnt[e]) for e in engs]

    def sep(self, eng):
        if eng == "a":
            self.rec("a", lambda e: e.nop())
        elif eng == "t":
            self.rec("t", lambda e: e.nop())
        else:
            self.rec(eng, lambda e: e.engine_nop())

    def barrier_w(self):
        return [(e, self.cnt[e]) for e in ("v", "a", "t", "p")]

    def emit(self, eng_name, e, sems):
        last_wait = {}
        for fn, waits, inc in self.ops[eng_name]:
            for sem, val in waits:
                if val > 0 and last_wait.get(sem, -1) < val:
                    e.wait_ge(sems[sem], val)
                    last_wait[sem] = val
            ret = fn(e)
            if inc is not None:
                ret.then_inc(sems[inc], 16 if inc == "q" else 1)


def build_program(dbg=False):
    nc = bass.Bass("TRN2", target_bir_lowering=False,
                   detect_race_conditions=False)

    x_t = nc.dram_tensor("x_t", [D, T], F32, kind="ExternalInput").ap()
    wdtT = nc.dram_tensor("wdtT", [P, DH * D], F32, kind="ExternalInput").ap()
    wbcT = nc.dram_tensor("wbcT", [P, DH * 32], F32, kind="ExternalInput").ap()
    alog = nc.dram_tensor("alog", [P, DH * N], F32, kind="ExternalInput").ap()
    bdt_in = nc.dram_tensor("bdt", [P, DH], F32, kind="ExternalInput").ap()
    dpar_in = nc.dram_tensor("dpar", [P, DH], F32, kind="ExternalInput").ap()
    convw_in = nc.dram_tensor("convw", [P, HH * 3], F32, kind="ExternalInput").ap()
    convb_in = nc.dram_tensor("convb", [P, HH], F32, kind="ExternalInput").ap()
    idn_in = nc.dram_tensor("idn", [P, P], F32, kind="ExternalInput").ap()
    w1T_in = nc.dram_tensor("w1T", [P, DH * HID], F32, kind="ExternalInput").ap()
    w2T_in = nc.dram_tensor("w2T", [P, HH * D], F32, kind="ExternalInput").ap()
    out_t = nc.dram_tensor("out_t", [D, T], F32, kind="ExternalOutput").ap()
    x1sc = nc.dram_tensor("x1sc", [D, T], F32, kind="ExternalOutput" if dbg else "Internal").ap()
    if dbg:
        dbg_dt = nc.dram_tensor("dbg_dt", [P, DH * 91], F32, kind="ExternalOutput").ap()
        dbg_a = nc.dram_tensor("dbg_a", [P, NSEG * 1280], F32, kind="ExternalOutput").ap()
        dbg_ast = nc.dram_tensor("dbg_ast", [P, NSEG * P], F32, kind="ExternalOutput").ap()
        dbg_uh = nc.dram_tensor("dbg_uh", [P, NSEG * 1280], F32, kind="ExternalOutput").ap()
        dbg_y = nc.dram_tensor("dbg_y", [P, DH * 91], F32, kind="ExternalOutput").ap()
        dbg_bp = nc.dram_tensor("dbg_bp", [P, N * 91], F32, kind="ExternalOutput").ap()
        dbg_cp = nc.dram_tensor("dbg_cp", [P, N * 91], F32, kind="ExternalOutput").ap()
        dbg_carry = nc.dram_tensor("dbg_carry", [P, P], F32, kind="ExternalOutput").ap()
        dbg_c0 = nc.dram_tensor("dbg_c0", [P, P], F32, kind="ExternalOutput").ap()
        dbg_u = nc.dram_tensor("dbg_u", [P, NSEG * 1280], F32, kind="ExternalOutput").ap()
        dbg_scl = nc.dram_tensor("dbg_scl", [P, 16], F32, kind="ExternalOutput").ap()
        dbgf_w1q = nc.dram_tensor("dbgf_w1q", [P, DH * HID], BF16, kind="ExternalOutput").ap()
        dbgf_x1q = nc.dram_tensor("dbgf_x1q", [P, DH * (T2 + 2)], BF16, kind="ExternalOutput").ap()
        dbgf_h1s = nc.dram_tensor("dbgf_h1s", [P, HH * T2], F32, kind="ExternalOutput").ap()
        dbgf_h1q = nc.dram_tensor("dbgf_h1q", [P, HH * T2], BF16, kind="ExternalOutput").ap()
        dbgf_rows = nc.dram_tensor("dbgf_rows", [1, 2048], F32, kind="ExternalOutput").ap()
        dbgf_fsb = nc.dram_tensor("dbgf_fsb", [P, T2 + 2], F32, kind="ExternalOutput").ap()
        dbgf_consts = nc.dram_tensor("dbgf_consts", [1, 16], F32, kind="ExternalOutput").ap()
    w1qsc = nc.dram_tensor("w1qsc", [P, DH * HID], BF16, kind="Internal").ap()
    w2qsc = nc.dram_tensor("w2qsc", [P, DH * HID], BF16, kind="Internal").ap()

    x_v = x_t.rearrange("(k p) t -> p k t", p=P)
    x1_v = x1sc.rearrange("(k p) t -> p k t", p=P)
    out_v = out_t.rearrange("(k p) t -> p k t", p=P)

    S = Sched()
    with contextlib.ExitStack() as top:
        sems = {n: top.enter_context(nc.semaphore("sem_" + n))
                for n in ("v", "a", "t", "p", "q")}

        def sb(name, shape, dt=F32, stack=top):
            return stack.enter_context(nc.sbuf_tensor(name, shape, dt))

        def pt(name, shape, stack=top):
            return stack.enter_context(nc.psum_tensor(name, shape, F32))

        consts = sb("consts", [1, 16])
        ones128 = sb("ones128", [P, 1])
        ones1 = sb("ones1", [1, P])
        A_res = sb("A_res", [P, DH * N])
        bdt_s = sb("bdt_s", [P, DH])
        dp1_s = sb("dp1_s", [P, DH])
        convw_s = sb("convw_s", [P, HH * 3])
        convb_s = sb("convb_s", [P, HH])
        idn_s = sb("idn_s", [P, P])
        cb = sb("cb", [P, 8])

        with contextlib.ExitStack() as ph0:
            wbuf = sb("wbuf", [P, DH * HID], F32, ph0)
            wq = wbuf.ap().bitcast(BF16)[:, 0:DH * HID]
            accs = sb("accs", [P, 8], F32, ph0)
            asc = sb("asc", [P, 4096], F32, ph0)
            rowt = sb("rowt", [1, 8], F32, ph0)
            ps0 = pt("ps0", [1, 512], ph0)
            psb0 = pt("psb0", [P, 512], ph0)

            for dst, src in ((A_res, alog), (bdt_s, bdt_in), (dp1_s, dpar_in),
                             (convw_s, convw_in), (convb_s, convb_in),
                             (idn_s, idn_in)):
                S.rec("s", (lambda d, sr: lambda e: e.dma_start(out=d[:], in_=sr))(dst, src),
                      inc="q")
            S.rec("v", lambda e: e.memset(ones128[:], 1.0))
            S.rec("v", lambda e: e.memset(ones1[:], 1.0))
            S.rec("v", lambda e: e.memset(cb[:, 0:1], SC_DT_BIAS))
            S.rec("v", lambda e: e.memset(cb[:, 1:2], 1.0))
            S.rec("v", lambda e: e.memset(cb[:, 2:3], 1e-5))
            S.rec("v", lambda e: e.memset(cb[:, 3:4], math.log(20.0)))
            S.rec("v", lambda e: e.memset(consts[:], 0.0), inc="v")
            v_consts = S.cnt["v"]
            S.rec("a", lambda e: e.activation(out=asc[:, 0:DH * N], in_=A_res[:],
                                              func=AF.Tanh, scale=SC5_SCALE),
                  waits=S.w("q") + [("v", v_consts)], inc="a")
            S.sep("a")
            S.rec("a", lambda e: e.activation(out=A_res[:], in_=asc[:, 0:DH * N],
                                              func=AF.Exp, scale=10.0), inc="a")
            S.rec("v", lambda e: e.tensor_scalar(out=A_res[:], in0=A_res[:],
                                                 scalar1=-1.0, scalar2=None,
                                                 op0=ALU.mult),
                  waits=S.w("a"), inc="v")
            S.rec("v", lambda e: e.tensor_scalar(out=dp1_s[:], in0=dp1_s[:],
                                                 scalar1=1.0, scalar2=None,
                                                 op0=ALU.add), inc="v")

            for wi, (wsrc, wdst, cslot) in enumerate(
                    ((w1T_in, w1qsc, 0), (w2T_in, w2qsc, 1))):
                S.rec("s", (lambda ws: lambda e: e.dma_start(out=wbuf[:], in_=ws))(wsrc),
                      waits=S.w("v"), inc="q")
                qw = S.cnt["q"]
                for sl in range(8):
                    S.rec("a", (lambda sl: lambda e: e.activation(
                        out=asc[:], in_=wbuf[:, sl * 4096:(sl + 1) * 4096],
                        func=AF.Abs, accum_out=accs[:, sl:sl + 1]))(sl),
                        waits=[("q", qw)], inc="a")
                S.rec("v", lambda e: e.tensor_reduce(out=accs[:, 0:1], in_=accs[:, 0:8],
                                                     axis=AX.X, op=ALU.add),
                      waits=S.w("a"), inc="v")
                S.rec("t", lambda e: nc.tensor.matmul(ps0[0:1, 0:1], ones128[:],
                                                      accs[:, 0:1], start=True, stop=True),
                      waits=S.w("v"), inc="t")
                S.rec("v", lambda e: e.tensor_scalar(out=rowt[0:1, 0:1], in0=ps0[0:1, 0:1],
                                                     scalar1=1.0 / (HID * D), scalar2=1e-5,
                                                     op0=ALU.mult, op1=ALU.max),
                      waits=S.w("t"), inc="v")
                S.sep("v")
                S.rec("v", (lambda cs: lambda e: e.tensor_scalar(
                    out=consts[0:1, cs:cs + 1], in0=rowt[0:1, 0:1],
                    scalar1=1.0 / 127.0, scalar2=None, op0=ALU.mult))(cslot), inc="v")
                S.rec("v", lambda e: e.reciprocal(out=rowt[0:1, 1:2], in_=rowt[0:1, 0:1]),
                      inc="v")
                S.rec("t", lambda e: nc.tensor.matmul(psb0[:, 0:1], ones1[:],
                                                      rowt[0:1, 1:2], start=True, stop=True),
                      waits=S.w("v"), inc="t")
                qprev = S.cnt["q"]
                for sl in range(8):
                    S.rec("v", (lambda sl: lambda e: e.tensor_scalar(
                        out=asc[:], in0=wbuf[:, sl * 4096:(sl + 1) * 4096],
                        scalar1=psb0[:, 0:1], scalar2=None, op0=ALU.mult))(sl),
                        waits=S.w("t") + [("q", qprev)])
                    S.sep("v")
                    S.rec("v", lambda e: e.tensor_scalar(
                        out=asc[:], in0=asc[:], scalar1=MAGIC, scalar2=MAGIC,
                        op0=ALU.add, op1=ALU.subtract))
                    S.sep("v")
                    S.rec("v", (lambda sl: lambda e: e.tensor_scalar(
                        out=wq[:, sl * 4096:(sl + 1) * 4096], in0=asc[:],
                        scalar1=1.0, scalar2=-1.0, op0=ALU.min, op1=ALU.max))(sl),
                        inc="v")
                S.rec("s", (lambda wd: lambda e: e.dma_start(out=wd, in_=wq))(wdst),
                      waits=S.w("v"), inc="q")
            q_wq_done = S.cnt["q"]

        with contextlib.ExitStack() as ph1:
            wdt_s = sb("wdt_s", [P, DH * D], F32, ph1)
            wbc_s = sb("wbc_s", [P, DH * 32], F32, ph1)
            a_buf = sb("a_buf", [P, NSEG * 1280], F32, ph1)
            uh_buf = sb("uh_buf", [P, NSEG * 1280], F32, ph1)
            scr = sb("scr", [P, 3840], F32, ph1)
            x_buf = sb("x_buf", [P, DH * 91], F32, ph1)
            r_buf = sb("r_buf", [P, DH * 91], F32, ph1)
            va_buf = sb("va_buf", [P, DH * 91], F32, ph1)
            e_buf = sb("e_buf", [P, DH * 91], F32, ph1)
            dt_buf = sb("dt_buf", [P, DH * 91], F32, ph1)
            z_buf = sb("z_buf", [P, DH * 91], F32, ph1)
            ast_buf = sb("ast_buf", [P, NSEG * P], F32, ph1)
            bp_rep = sb("bp_rep", [P, N * 91], F32, ph1)
            c_rep = sb("c_rep", [P, N * 91], F32, ph1)
            bp_row = sb("bp_row", [1, N * 91], F32, ph1)
            c_row = sb("c_row", [1, N * 91], F32, ph1)
            y_buf = sb("y_buf", [P, DH * 91], F32, ph1)
            x1_buf = sb("x1_buf", [P, DH * 91], F32, ph1)
            carry = sb("carry", [P, P], F32, ph1)
            itmp = sb("itmp", [P, P], F32, ph1)
            w_tmp = sb("w_tmp", [P, 720], F32, ph1)
            w_tmp2 = sb("w_tmp2", [P, 720], F32, ph1)
            sqs = sb("sqs", [P, 1], F32, ph1)
            sq_scr = sb("sq_scr", [P, P], F32, ph1)
            scl = sb("scl", [1, 8], F32, ph1)
            lrow = sb("lrow", [1, 4 * 91], F32, ph1)
            bc_sb = sb("bc_sb", [32, 91], F32, ph1)

            psDT = [pt("psDT0", [P, 512], ph1), pt("psDT1", [P, 512], ph1)]
            psBC = pt("psBC", [32, 512], ph1)
            psREP = pt("psREP", [P, 512], ph1)
            psN = pt("psN", [1, 512], ph1)
            psB = pt("psB", [P, 512], ph1)
            psLN = pt("psLN", [1, 512], ph1)
            psLN2 = pt("psLN2", [1, 512], ph1)

            wdt_v = wdt_s.ap().rearrange("p (k c) -> p k c", k=DH)
            wbc_v = wbc_s.ap().rearrange("p (k c) -> p k c", k=DH)
            a_4d = a_buf.ap().rearrange("p (k s j) -> p k s j", k=NSEG, s=P)
            uh_4d = uh_buf.ap().rearrange("p (k s j) -> p k s j", k=NSEG, s=P)

            S.rec("s", lambda e: e.dma_start(out=wdt_s[:], in_=wdtT),
                  waits=S.barrier_w(), inc="q")
            S.rec("s", lambda e: e.dma_start(out=wbc_s[:], in_=wbcT), inc="q")
            S.rec("v", lambda e: e.memset(carry[:], 0.0), inc="v")

            def s_chain(hview):
                S.rec("a", (lambda hv: lambda e: e.activation(
                    out=sq_scr[:], in_=hv, func=AF.Square,
                    accum_out=sqs[:, 0:1]))(hview), waits=S.w("v"), inc="a")
                S.rec("t", lambda e: nc.tensor.matmul(psN[0:1, 0:1], ones128[:],
                                                      sqs[:, 0:1], start=True, stop=True),
                      waits=S.w("a"))
                S.rec("t", lambda e: nc.tensor.matmul(psLN[0:1, 0:1], ones128[:],
                                                      sqs[:, 0:1], start=True, stop=True),
                      inc="t")
                S.rec("a", lambda e: e.activation(out=scl[0:1, 0:1], in_=psN[0:1, 0:1],
                                                  func=AF.Ln), waits=S.w("t"), inc="a")
                S.sep("a")
                S.rec("a", lambda e: e.activation(out=scl[0:1, 1:2], in_=scl[0:1, 0:1],
                                                  func=AF.Exp, scale=-0.5,
                                                  bias=cb[0:1, 3:4]))
                S.sep("a")
                S.rec("a", lambda e: e.activation(out=scl[0:1, 2:3], in_=scl[0:1, 1:2],
                                                  func=AF.Relu, scale=-1.0,
                                                  bias=cb[0:1, 1:2]))
                S.sep("a")
                S.rec("a", lambda e: e.activation(out=scl[0:1, 4:5], in_=scl[0:1, 2:3],
                                                  func=AF.Relu, scale=-1.0,
                                                  bias=cb[0:1, 1:2]))
                S.sep("a")
                S.rec("a", lambda e: e.activation(out=scl[0:1, 5:6], in_=scl[0:1, 4:5],
                                                  func=AF.Copy), inc="a")
                S.rec("t", lambda e: nc.tensor.matmul(psB[:, 0:1], ones1[:],
                                                      scl[0:1, 4:5], start=True, stop=True),
                      waits=S.w("a"))
                S.rec("t", lambda e: nc.tensor.matmul(psLN2[0:1, 0:1], ones1[0:1, 0:1],
                                                      scl[0:1, 4:5], start=True, stop=True),
                      inc="t")
                S.sep("v")
                S.rec("v", (lambda hv: lambda e: e.tensor_scalar(
                    out=hv, in0=hv, scalar1=psB[:, 0:1], scalar2=None,
                    op0=ALU.mult))(hview), waits=S.w("t"), inc="v")

            prev_gather_q = 0
            prev_Vx1 = S.cnt["v"]

            for m in range(NCH):
                tok0 = 0 if m == 0 else 90 * m + 1
                Sc = 91 if m == 0 else 90
                xo = 1 if m == 0 else 0
                a_chunk0 = S.cnt["a"]

                S.rec("s", (lambda t0, sc: lambda e: e.dma_start(
                    out=x_buf[:, 0:DH * sc].rearrange("p (k c) -> p k c", k=DH),
                    in_=x_v[:, :, t0:t0 + sc]))(tok0, Sc),
                    waits=[("v", prev_Vx1)], inc="q")
                q_x = S.cnt["q"]

                a_base = S.cnt["a"]
                for dd in range(DH):
                    w0 = [("q", q_x)]
                    if dd >= 2:
                        w0.append(("a", a_base + (dd - 1)))
                    else:
                        w0.append(("a", a_chunk0))
                    for k in range(DH):
                        S.rec("t", (lambda dd, k, sc: lambda e: nc.tensor.matmul(
                            psDT[dd % 2][:, 0:sc], wdt_v[:, k, dd * P:(dd + 1) * P],
                            x_buf[:, 0:DH * sc].rearrange("p (q c) -> p q c", q=DH)[:, k],
                            start=(k == 0), stop=(k == DH - 1)))(dd, k, Sc),
                            waits=w0 if k == 0 else (),
                            inc="t" if k == DH - 1 else None)
                    tq = S.cnt["t"]
                    S.rec("a", (lambda dd, sc: lambda e: e.activation(
                        out=r_buf[:, dd * sc:(dd + 1) * sc], in_=psDT[dd % 2][:, 0:sc],
                        func=AF.Relu, bias=bdt_s[:, dd:dd + 1]))(dd, Sc),
                        waits=[("t", tq)])
                    S.rec("a", (lambda dd, sc: lambda e: e.activation(
                        out=va_buf[:, dd * sc:(dd + 1) * sc], in_=psDT[dd % 2][:, 0:sc],
                        func=AF.Abs, bias=bdt_s[:, dd:dd + 1]))(dd, Sc), inc="a")

                for k in range(DH):
                    S.rec("t", (lambda k, sc: lambda e: nc.tensor.matmul(
                        psBC[0:32, 0:sc], wbc_v[:, k, :],
                        x_buf[:, 0:DH * sc].rearrange("p (q c) -> p q c", q=DH)[:, k],
                        start=(k == 0), stop=(k == DH - 1)))(k, Sc),
                        waits=[("q", max(prev_gather_q, q_x))] if k == 0 else (),
                        inc="t" if k == DH - 1 else None)
                t_bc = S.cnt["t"]

                S.rec("a", (lambda sc: lambda e: e.activation(
                    out=e_buf[:, 0:DH * sc], in_=va_buf[:, 0:DH * sc],
                    func=AF.Exp, scale=-1.0))(Sc))
                S.rec("a", (lambda sc: lambda e: e.activation(
                    out=va_buf[:, 0:DH * sc], in_=e_buf[:, 0:DH * sc],
                    func=AF.Ln, bias=cb[:, 1:2]))(Sc), inc="a")
                S.rec("v", (lambda sc: lambda e: e.tensor_tensor(
                    out=r_buf[:, 0:DH * sc], in0=r_buf[:, 0:DH * sc],
                    in1=va_buf[:, 0:DH * sc], op=ALU.add))(Sc),
                    waits=S.w("a"), inc="v")
                S.rec("a", (lambda sc: lambda e: e.activation(
                    out=e_buf[:, 0:DH * sc], in_=r_buf[:, 0:DH * sc],
                    func=AF.Tanh, scale=SC_DT_SCALE, bias=cb[:, 0:1]))(Sc),
                    waits=S.w("v"), inc="a")
                S.rec("v", (lambda sc: lambda e: e.tensor_scalar(
                    out=dt_buf[:, 0:DH * sc], in0=e_buf[:, 0:DH * sc],
                    scalar1=SC_DT_RNG, scalar2=SC_DT_CTR,
                    op0=ALU.mult, op1=ALU.add))(Sc),
                    waits=S.w("a"), inc="v")
                S.sep("v")
                S.rec("v", (lambda sc: lambda e: e.tensor_tensor(
                    out=z_buf[:, 0:DH * sc].rearrange("p (d t) -> p d t", d=DH),
                    in0=dt_buf[:, 0:DH * sc].rearrange("p (d t) -> p d t", d=DH),
                    in1=x_buf[:, 0:DH * sc].rearrange("p (d t) -> p d t", d=DH),
                    op=ALU.mult))(Sc), inc="v")
                for dd in range(DH):
                    S.rec("v", (lambda dd, sc, x0: lambda e: e.tensor_tensor(
                        out=a_4d[:, :, dd * N:(dd + 1) * N, :],
                        in0=dt_buf[:, 0:DH * sc].rearrange("p (d t) -> p d t", d=DH)
                            [:, dd, x0:x0 + 90].rearrange("p (k j) -> p k j", j=10)
                            .unsqueeze(2).broadcast_to([P, NSEG, N, 10]),
                        in1=A_res[:].rearrange("p (d n) -> p d n", d=DH)[:, dd]
                            .unsqueeze(1).broadcast_to([P, NSEG, N])
                            .unsqueeze(3).broadcast_to([P, NSEG, N, 10]),
                        op=ALU.mult))(dd, Sc, xo),
                        inc="v" if dd == DH - 1 else None)
                v_dta = S.cnt["v"]
                for qq in range(4):
                    lo, hi = qq * 2880, (qq + 1) * 2880
                    S.rec("a", (lambda lo, hi: lambda e: e.activation(
                        out=scr[:, 0:hi - lo], in_=a_buf[:, lo:hi],
                        func=AF.Tanh, scale=SC8_SCALE))(lo, hi),
                        waits=[("v", v_dta)] if qq == 0 else ())
                    S.rec("a", (lambda lo, hi: lambda e: e.activation(
                        out=a_buf[:, lo:hi], in_=scr[:, 0:hi - lo],
                        func=AF.Exp, scale=16.0))(lo, hi),
                        inc="a" if qq == 3 else None)
                a_ready = S.cnt["a"]
                S.rec("v", lambda e: e.tensor_copy(
                    out=ast_buf[:].rearrange("p (k s) -> p k s", k=NSEG),
                    in_=a_4d[:, :, :, 0:1].squeeze(3)),
                    waits=[("a", a_ready)])
                S.sep("v")
                S.rec("v", lambda e: e.memset(a_4d[:, :, :, 0:1], 0.0), inc="v")

                S.rec("a", (lambda sc: lambda e: e.activation(
                    out=bc_sb[0:32, 0:sc], in_=psBC[0:32, 0:sc],
                    func=AF.Copy))(Sc), waits=[("t", t_bc)], inc="a")
                a_bc = S.cnt["a"]
                S.rec("s", (lambda sc: lambda e: e.dma_start(
                    out=bp_row[0:1, 0:N * sc], in_=bc_sb[0:N, 0:sc]))(Sc),
                    waits=[("a", a_bc)], inc="q")
                S.rec("s", (lambda sc: lambda e: e.dma_start(
                    out=c_row[0:1, 0:N * sc], in_=bc_sb[N:32, 0:sc]))(Sc), inc="q")
                q_gather = S.cnt["q"]
                prev_gather_q = q_gather
                tot = N * Sc
                parts = [(i, min(i + 448, tot)) for i in range(0, tot, 448)]
                for rep, row in ((bp_rep, bp_row), (c_rep, c_row)):
                    for lo, hi in parts:
                        S.rec("t", (lambda row, lo, hi: lambda e: nc.tensor.matmul(
                            psREP[:, 0:hi - lo], ones1[:], row[0:1, lo:hi],
                            start=True, stop=True))(row, lo, hi),
                            waits=[("q", q_gather), ("a", S.cnt["a"]),
                                   ("v", prev_Vx1)], inc="t")
                        S.rec("a", (lambda rep, lo, hi: lambda e: e.activation(
                            out=rep[:, lo:hi], in_=psREP[:, 0:hi - lo],
                            func=AF.Copy))(rep, lo, hi),
                            waits=S.w("t"), inc="a")
                a_reps = S.cnt["a"]

                for dd in range(DH):
                    S.rec("v", (lambda dd, sc, x0: lambda e: e.tensor_tensor(
                        out=uh_4d[:, :, dd * N:(dd + 1) * N, :],
                        in0=z_buf[:, 0:DH * sc].rearrange("p (d t) -> p d t", d=DH)
                            [:, dd, x0:x0 + 90].rearrange("p (k j) -> p k j", j=10)
                            .unsqueeze(2).broadcast_to([P, NSEG, N, 10]),
                        in1=bp_rep[:, 0:N * sc].rearrange("p (n t) -> p n t", n=N)
                            [:, :, x0:x0 + 90].rearrange("p n (k j) -> p k n j", j=10),
                        op=ALU.mult))(dd, Sc, xo),
                        waits=[("a", a_reps)] if dd == 0 else (),
                        inc="v" if dd == DH - 1 else None)

                if m == 0:
                    S.rec("v", lambda e: e.tensor_tensor(
                        out=carry[:].rearrange("p (d n) -> p d n", d=DH),
                        in0=z_buf[:, 0:DH * 91].rearrange("p (d t) -> p d t", d=DH)
                            [:, :, 0:1].broadcast_to([P, DH, N]),
                        in1=bp_rep[:, 0:N * 91].rearrange("p (n t) -> p n t", n=N)
                            [:, :, 0:1].transpose([0, 2, 1]).broadcast_to([P, DH, N]),
                        op=ALU.mult), inc="v")
                    s_chain(carry[:])
                    if dbg:
                        S.rec("s", lambda e: e.dma_start(out=dbg_c0, in_=carry[:]),
                              waits=S.w("v"), inc="q")
                        S.rec("s", lambda e: e.dma_start(out=dbg_u, in_=uh_buf[:]), inc="q")
                        S.rec("v", lambda e: e.engine_nop(), waits=S.w("q"), inc="v")
                    S.rec("v", lambda e: e.memset(y_buf[:, 0:DH * 91], 0.0), inc="v")
                    S.rec("v", lambda e: e.tensor_tensor(
                        out=itmp[:].rearrange("p (q d) -> p q d", q=N),
                        in0=carry[:].rearrange("p (d n) -> p d n", d=DH).transpose([0, 2, 1]),
                        in1=c_rep[:, 0:N * 91].rearrange("p (q t) -> p q t", q=N)
                            [:, :, 0:1].broadcast_to([P, N, DH]),
                        op=ALU.mult))
                    S.sep("v")
                    S.sep("v")
                    S.rec("v", lambda e: e.tensor_reduce(
                        out=y_buf[:, 0:DH * 91].rearrange("p (d t) -> p d t", d=DH)[:, :, 0:1].squeeze(2),
                        in_=itmp[:].rearrange("p (q d) -> p q d", q=N).transpose([0, 2, 1]),
                        axis=AX.X, op=ALU.add), inc="v")

                for k in range(NSEG):
                    hprev = carry[:] if k == 0 else uh_4d[:, k - 1, :, 9:10].squeeze(2)
                    S.sep("v")
                    S.rec("v", (lambda k, hp: lambda e: e.tensor_tensor(
                        out=itmp[:], in0=ast_buf[:, k * P:(k + 1) * P],
                        in1=hp, op=ALU.mult))(k, hprev))
                    S.sep("v")
                    S.rec("v", (lambda k: lambda e: e.tensor_tensor(
                        out=uh_4d[:, k, :, 0:1].squeeze(2),
                        in0=uh_4d[:, k, :, 0:1].squeeze(2),
                        in1=itmp[:], op=ALU.add))(k))
                    S.sep("v")
                    S.rec("v", (lambda k: lambda e: e.tensor_tensor_scan(
                        out=uh_buf[:, k * 1280:(k + 1) * 1280],
                        data0=a_buf[:, k * 1280:(k + 1) * 1280],
                        data1=uh_buf[:, k * 1280:(k + 1) * 1280],
                        initial=0.0, op0=ALU.mult, op1=ALU.add))(k), inc="v")
                    s_chain(uh_4d[:, k, :, 9:10].squeeze(2))
                    if dbg and m == 0 and k == 0:
                        S.rec("s", lambda e: e.dma_start(out=dbg_scl[0:1, 0:8], in_=scl[0:1, 0:8]),
                              waits=S.w("v"), inc="q")
                        S.rec("s", lambda e: e.dma_start(out=dbg_scl[:, 8:9], in_=sqs[:, 0:1],
                              allow_slow_non_contiguous=True), inc="q")
                        S.rec("v", lambda e: e.engine_nop(), waits=S.w("q"), inc="v")
                S.sep("v")
                S.rec("v", lambda e: e.tensor_copy(
                    out=carry[:], in_=uh_4d[:, NSEG - 1, :, 9:10].squeeze(2)), inc="v")

                if m == 0 and dbg:
                    S.rec("s", lambda e: e.dma_start(out=dbg_uh, in_=uh_buf[:]),
                          waits=S.w("v"), inc="q")
                    S.rec("s", lambda e: e.dma_start(out=dbg_carry, in_=carry[:]),
                          inc="q")
                if m != 0:
                    S.rec("v", lambda e: e.memset(y_buf[:, 0:DH * 90], 0.0), inc="v")
                def ymul_op(n, sc):
                    wt = w_tmp if n % 2 == 0 else w_tmp2
                    S.rec("v", (lambda n, sc, wt: lambda e: e.tensor_tensor(
                        out=wt[:].rearrange("p (k d j) -> p k d j", k=NSEG, d=DH),
                        in0=uh_4d[:, :, slice(n, P, N), :],
                        in1=c_rep[:, 0:N * sc].rearrange("p (q t) -> p q t", q=N)
                            [:, n, (1 if sc == 91 else 0):(1 if sc == 91 else 0) + 90]
                            .rearrange("p (k j) -> p k j", j=10)
                            .unsqueeze(2).broadcast_to([P, NSEG, DH, 10]),
                        op=ALU.mult))(n, sc, wt))

                def yadd_op(n, sc, x0, last):
                    wt = w_tmp if n % 2 == 0 else w_tmp2
                    S.rec("v", (lambda n, sc, x0, wt: lambda e: e.tensor_tensor(
                        out=y_buf[:, 0:DH * sc].rearrange("p (d t) -> p d t", d=DH)
                            [:, :, x0:x0 + 90].rearrange("p d (k j) -> p k d j", j=10),
                        in0=y_buf[:, 0:DH * sc].rearrange("p (d t) -> p d t", d=DH)
                            [:, :, x0:x0 + 90].rearrange("p d (k j) -> p k d j", j=10),
                        in1=wt[:].rearrange("p (k d j) -> p k d j", k=NSEG, d=DH),
                        op=ALU.add))(n, sc, x0, wt),
                        inc="v" if last else None)

                ymul_op(0, Sc)
                for n in range(1, N):
                    ymul_op(n, Sc)
                    yadd_op(n - 1, Sc, xo, False)
                S.sep("v")
                yadd_op(N - 1, Sc, xo, True)
                if m == 0 and dbg:
                    S.rec("s", lambda e: e.dma_start(out=dbg_y, in_=y_buf[:]),
                          waits=S.w("v"), inc="q")
                    S.rec("s", lambda e: e.dma_start(out=dbg_dt, in_=dt_buf[:]), inc="q")
                    S.rec("s", lambda e: e.dma_start(out=dbg_a, in_=a_buf[:]), inc="q")
                    S.rec("s", lambda e: e.dma_start(out=dbg_ast, in_=ast_buf[:]), inc="q")
                    S.rec("s", lambda e: e.dma_start(out=dbg_bp, in_=bp_rep[:]), inc="q")
                    S.rec("s", lambda e: e.dma_start(out=dbg_cp, in_=c_rep[:]), inc="q")
                    S.rec("v", lambda e: e.engine_nop(), waits=S.w("q"), inc="v")
                S.sep("v")
                for dd in range(DH):
                    S.rec("v", (lambda dd, sc: lambda e: e.scalar_tensor_tensor(
                        out=y_buf[:, dd * sc:(dd + 1) * sc],
                        in0=x_buf[:, dd * sc:(dd + 1) * sc],
                        scalar=dp1_s[:, dd:dd + 1],
                        in1=y_buf[:, dd * sc:(dd + 1) * sc],
                        op0=ALU.mult, op1=ALU.add))(dd, Sc),
                        inc="v" if dd == DH - 1 else None)
                v_u1 = S.cnt["v"]
                S.rec("a", (lambda sc: lambda e: e.activation(
                    out=scr[:, 0:DH * sc], in_=y_buf[:, 0:DH * sc], func=AF.Square))(Sc),
                    waits=[("v", v_u1)], inc="a")
                a_sq = S.cnt["a"]
                for dd in range(DH):
                    S.rec("t", (lambda dd, sc: lambda e: nc.tensor.matmul(
                        psLN[0:1, 0:sc], ones128[:], y_buf[:, dd * sc:(dd + 1) * sc],
                        start=(dd == 0), stop=(dd == DH - 1)))(dd, Sc),
                        waits=[("v", v_u1)] if dd == 0 else (),
                        inc="t" if dd == DH - 1 else None)
                for dd in range(DH):
                    S.rec("t", (lambda dd, sc: lambda e: nc.tensor.matmul(
                        psLN2[0:1, 0:sc], ones128[:], scr[:, dd * sc:(dd + 1) * sc],
                        start=(dd == 0), stop=(dd == DH - 1)))(dd, Sc),
                        waits=[("a", a_sq)] if dd == 0 else (),
                        inc="t" if dd == DH - 1 else None)
                t_ln = S.cnt["t"]
                S.rec("v", (lambda sc: lambda e: e.tensor_scalar(
                    out=lrow[0:1, 0:sc], in0=psLN[0:1, 0:sc],
                    scalar1=1.0 / D, scalar2=None, op0=ALU.mult))(Sc),
                    waits=[("t", t_ln)])
                S.rec("v", (lambda sc: lambda e: e.tensor_scalar(
                    out=lrow[0:1, 91:91 + sc], in0=psLN2[0:1, 0:sc],
                    scalar1=1.0 / D, scalar2=None, op0=ALU.mult))(Sc))
                S.rec("v", (lambda sc: lambda e: e.tensor_tensor(
                    out=lrow[0:1, 182:182 + sc], in0=lrow[0:1, 0:sc],
                    in1=lrow[0:1, 0:sc], op=ALU.mult))(Sc))
                S.sep("v")
                S.rec("v", (lambda sc: lambda e: e.tensor_tensor(
                    out=lrow[0:1, 91:91 + sc], in0=lrow[0:1, 91:91 + sc],
                    in1=lrow[0:1, 182:182 + sc], op=ALU.subtract))(Sc), inc="v")
                S.rec("a", (lambda sc: lambda e: e.activation(
                    out=lrow[0:1, 182:182 + sc], in_=lrow[0:1, 91:91 + sc],
                    func=AF.Ln, bias=cb[0:1, 2:3]))(Sc), waits=S.w("v"))
                S.sep("a")
                S.rec("a", (lambda sc: lambda e: e.activation(
                    out=lrow[0:1, 273:273 + sc], in_=lrow[0:1, 182:182 + sc],
                    func=AF.Exp, scale=-0.5))(Sc), inc="a")
                S.rec("t", (lambda sc: lambda e: nc.tensor.matmul(
                    psREP[:, 0:sc], ones1[:], lrow[0:1, 0:sc],
                    start=True, stop=True))(Sc),
                    waits=S.w("a") + [("v", prev_Vx1)], inc="t")
                S.rec("t", (lambda sc: lambda e: nc.tensor.matmul(
                    psB[:, 0:sc], ones1[:], lrow[0:1, 273:273 + sc],
                    start=True, stop=True))(Sc), inc="t")
                t_rep = S.cnt["t"]
                S.rec("v", (lambda sc: lambda e: e.tensor_tensor(
                    out=x1_buf[:, 0:DH * sc].rearrange("p (d t) -> p d t", d=DH),
                    in0=y_buf[:, 0:DH * sc].rearrange("p (d t) -> p d t", d=DH),
                    in1=psREP[:, 0:sc].unsqueeze(1).broadcast_to([P, DH, sc]),
                    op=ALU.subtract))(Sc),
                    waits=[("t", t_rep), ("q", S.cnt["q"])])
                S.sep("v")
                S.rec("v", (lambda sc: lambda e: e.tensor_tensor(
                    out=x1_buf[:, 0:DH * sc].rearrange("p (d t) -> p d t", d=DH),
                    in0=x1_buf[:, 0:DH * sc].rearrange("p (d t) -> p d t", d=DH),
                    in1=psB[:, 0:sc].unsqueeze(1).broadcast_to([P, DH, sc]),
                    op=ALU.mult))(Sc), inc="v")
                prev_Vx1 = S.cnt["v"]
                S.rec("s", (lambda t0, sc: lambda e: e.dma_start(
                    out=x1_v[:, :, t0:t0 + sc],
                    in_=x1_buf[:, 0:DH * sc].rearrange("p (k c) -> p k c", k=DH)))(tok0, Sc),
                    waits=[("v", prev_Vx1)], inc="q")

            TL = 8191
            S.rec("s", lambda e: e.dma_start(
                out=x_buf[:, 0:DH].rearrange("p (k c) -> p k c", k=DH),
                in_=x_v[:, :, TL:TL + 1], allow_slow_non_contiguous=True),
                waits=[("v", prev_Vx1)], inc="q")
            q_x = S.cnt["q"]
            a_pre_tail = S.cnt["a"]
            for dd in range(DH):
                for k in range(DH):
                    S.rec("t", (lambda dd, k: lambda e: nc.tensor.matmul(
                        psDT[dd % 2][:, 0:1], wdt_v[:, k, dd * P:(dd + 1) * P],
                        x_buf[:, 0:DH].rearrange("p (q c) -> p q c", q=DH)[:, k],
                        start=(k == 0), stop=(k == DH - 1)))(dd, k),
                        waits=[("q", q_x), ("a", a_pre_tail)] if k == 0 else (),
                        inc="t" if k == DH - 1 else None)
                tq = S.cnt["t"]
                S.rec("a", (lambda dd: lambda e: e.activation(
                    out=r_buf[:, dd:dd + 1], in_=psDT[dd % 2][:, 0:1],
                    func=AF.Relu, bias=bdt_s[:, dd:dd + 1]))(dd), waits=[("t", tq)])
                S.rec("a", (lambda dd: lambda e: e.activation(
                    out=va_buf[:, dd:dd + 1], in_=psDT[dd % 2][:, 0:1],
                    func=AF.Abs, bias=bdt_s[:, dd:dd + 1]))(dd), inc="a")
            for k in range(DH):
                S.rec("t", (lambda k: lambda e: nc.tensor.matmul(
                    psBC[0:32, 0:1], wbc_v[:, k, :],
                    x_buf[:, 0:DH].rearrange("p (q c) -> p q c", q=DH)[:, k],
                    start=(k == 0), stop=(k == DH - 1)))(k),
                    waits=[("q", max(prev_gather_q, q_x))] if k == 0 else (),
                    inc="t" if k == DH - 1 else None)
            t_bc = S.cnt["t"]
            S.rec("a", lambda e: e.activation(out=e_buf[:, 0:DH], in_=va_buf[:, 0:DH],
                                              func=AF.Exp, scale=-1.0))
            S.rec("a", lambda e: e.activation(out=va_buf[:, 0:DH], in_=e_buf[:, 0:DH],
                                              func=AF.Ln, bias=cb[:, 1:2]), inc="a")
            S.rec("v", lambda e: e.tensor_tensor(out=r_buf[:, 0:DH], in0=r_buf[:, 0:DH],
                                                 in1=va_buf[:, 0:DH], op=ALU.add),
                  waits=S.w("a"), inc="v")
            S.rec("a", lambda e: e.activation(out=e_buf[:, 0:DH], in_=r_buf[:, 0:DH],
                                              func=AF.Tanh, scale=SC_DT_SCALE,
                                              bias=cb[:, 0:1]),
                  waits=S.w("v"), inc="a")
            S.rec("v", lambda e: e.tensor_scalar(out=dt_buf[:, 0:DH], in0=e_buf[:, 0:DH],
                                                 scalar1=SC_DT_RNG, scalar2=SC_DT_CTR,
                                                 op0=ALU.mult, op1=ALU.add),
                  waits=S.w("a"), inc="v")
            S.sep("v")
            S.rec("v", lambda e: e.tensor_tensor(
                out=a_buf[:, 0:P].rearrange("p (d n) -> p d n", d=DH),
                in0=dt_buf[:, 0:DH].unsqueeze(2).broadcast_to([P, DH, N]),
                in1=A_res[:].rearrange("p (d n) -> p d n", d=DH), op=ALU.mult),
                inc="v")
            S.rec("a", lambda e: e.activation(out=scr[:, 0:P], in_=a_buf[:, 0:P],
                                              func=AF.Tanh, scale=SC8_SCALE),
                  waits=S.w("v"))
            S.sep("a")
            S.rec("a", lambda e: e.activation(out=a_buf[:, 0:P], in_=scr[:, 0:P],
                                              func=AF.Exp, scale=16.0), inc="a")
            S.rec("a", lambda e: e.activation(out=bc_sb[0:32, 0:1], in_=psBC[0:32, 0:1],
                                              func=AF.Copy), waits=[("t", t_bc)], inc="a")
            a_bc = S.cnt["a"]
            S.rec("s", lambda e: e.dma_start(out=bp_row[0:1, 0:N], in_=bc_sb[0:N, 0:1]),
                  waits=[("a", a_bc)], inc="q")
            S.rec("s", lambda e: e.dma_start(out=c_row[0:1, 0:N], in_=bc_sb[N:32, 0:1]),
                  inc="q")
            q_gather = S.cnt["q"]
            S.rec("t", lambda e: nc.tensor.matmul(psREP[:, 0:N], ones1[:],
                                                  bp_row[0:1, 0:N], start=True, stop=True),
                  waits=[("q", q_gather), ("a", S.cnt["a"]), ("v", S.cnt["v"])], inc="t")
            S.rec("a", lambda e: e.activation(out=bp_rep[:, 0:N], in_=psREP[:, 0:N],
                                              func=AF.Copy), waits=S.w("t"), inc="a")
            S.rec("t", lambda e: nc.tensor.matmul(psREP[:, 0:N], ones1[:],
                                                  c_row[0:1, 0:N], start=True, stop=True),
                  waits=S.w("a"), inc="t")
            S.rec("a", lambda e: e.activation(out=c_rep[:, 0:N], in_=psREP[:, 0:N],
                                              func=AF.Copy), waits=S.w("t"), inc="a")
            S.rec("v", lambda e: e.tensor_tensor(out=itmp[:, 0:DH], in0=dt_buf[:, 0:DH],
                                                 in1=x_buf[:, 0:DH], op=ALU.mult),
                  waits=S.w("a"))
            S.sep("v")
            S.sep("v")
            S.sep("v")
            S.rec("v", lambda e: e.tensor_tensor(
                out=uh_buf[:, 0:P].rearrange("p (d n) -> p d n", d=DH),
                in0=itmp[:, 0:DH].unsqueeze(2).broadcast_to([P, DH, N]),
                in1=bp_rep[:, 0:N].unsqueeze(1).broadcast_to([P, DH, N]),
                op=ALU.mult))
            S.sep("v")
            S.sep("v")
            S.rec("v", lambda e: e.tensor_tensor(out=itmp[:], in0=a_buf[:, 0:P],
                                                 in1=carry[:], op=ALU.mult))
            S.sep("v")
            S.sep("v")
            S.rec("v", lambda e: e.tensor_tensor(out=uh_buf[:, 0:P], in0=uh_buf[:, 0:P],
                                                 in1=itmp[:], op=ALU.add), inc="v")
            S.rec("v", lambda e: e.memset(y_buf[:, 0:DH], 0.0), inc="v")
            S.sep("v")
            S.rec("v", lambda e: e.tensor_tensor(
                out=itmp[:].rearrange("p (q d) -> p q d", q=N),
                in0=uh_buf[:, 0:P].rearrange("p (d n) -> p d n", d=DH).transpose([0, 2, 1]),
                in1=c_rep[:, 0:N].unsqueeze(2).broadcast_to([P, N, DH]),
                op=ALU.mult))
            S.sep("v")
            S.sep("v")
            S.rec("v", lambda e: e.tensor_reduce(
                out=y_buf[:, 0:DH],
                in_=itmp[:].rearrange("p (q d) -> p q d", q=N).transpose([0, 2, 1]),
                axis=AX.X, op=ALU.add), inc="v")
            S.sep("v")
            for dd in range(DH):
                S.rec("v", (lambda dd: lambda e: e.scalar_tensor_tensor(
                    out=y_buf[:, dd:dd + 1], in0=x_buf[:, dd:dd + 1],
                    scalar=dp1_s[:, dd:dd + 1], in1=y_buf[:, dd:dd + 1],
                    op0=ALU.mult, op1=ALU.add))(dd),
                    inc="v" if dd == DH - 1 else None)
            v_u1 = S.cnt["v"]
            S.rec("a", lambda e: e.activation(out=scr[:, 0:DH], in_=y_buf[:, 0:DH],
                                              func=AF.Square),
                  waits=[("v", v_u1)], inc="a")
            a_sq = S.cnt["a"]
            for dd in range(DH):
                S.rec("t", (lambda dd: lambda e: nc.tensor.matmul(
                    psLN[0:1, 0:1], ones128[:], y_buf[:, dd:dd + 1],
                    start=(dd == 0), stop=(dd == DH - 1)))(dd),
                    waits=[("v", v_u1)] if dd == 0 else (),
                    inc="t" if dd == DH - 1 else None)
            for dd in range(DH):
                S.rec("t", (lambda dd: lambda e: nc.tensor.matmul(
                    psLN2[0:1, 0:1], ones128[:], scr[:, dd:dd + 1],
                    start=(dd == 0), stop=(dd == DH - 1)))(dd),
                    waits=[("a", a_sq)] if dd == 0 else (),
                    inc="t" if dd == DH - 1 else None)
            t_ln = S.cnt["t"]
            S.rec("v", lambda e: e.tensor_scalar(out=lrow[0:1, 0:1], in0=psLN[0:1, 0:1],
                                                 scalar1=1.0 / D, scalar2=None,
                                                 op0=ALU.mult), waits=[("t", t_ln)])
            S.rec("v", lambda e: e.tensor_scalar(out=lrow[0:1, 1:2], in0=psLN2[0:1, 0:1],
                                                 scalar1=1.0 / D, scalar2=None,
                                                 op0=ALU.mult))
            S.rec("v", lambda e: e.tensor_tensor(out=lrow[0:1, 2:3], in0=lrow[0:1, 0:1],
                                                 in1=lrow[0:1, 0:1], op=ALU.mult))
            S.sep("v")
            S.sep("v")
            S.rec("v", lambda e: e.tensor_tensor(out=lrow[0:1, 1:2], in0=lrow[0:1, 1:2],
                                                 in1=lrow[0:1, 2:3], op=ALU.subtract),
                  inc="v")
            S.rec("a", lambda e: e.activation(out=lrow[0:1, 2:3], in_=lrow[0:1, 1:2],
                                              func=AF.Ln, bias=cb[0:1, 2:3]), waits=S.w("v"))
            S.sep("a")
            S.rec("a", lambda e: e.activation(out=lrow[0:1, 3:4], in_=lrow[0:1, 2:3],
                                              func=AF.Exp, scale=-0.5), inc="a")
            S.rec("t", lambda e: nc.tensor.matmul(psREP[:, 0:1], ones1[:], lrow[0:1, 0:1],
                                                  start=True, stop=True),
                  waits=S.w("a") + [("v", S.cnt["v"])], inc="t")
            S.rec("t", lambda e: nc.tensor.matmul(psB[:, 0:1], ones1[:], lrow[0:1, 3:4],
                                                  start=True, stop=True), inc="t")
            t_rep = S.cnt["t"]
            S.rec("v", lambda e: e.tensor_tensor(
                out=x1_buf[:, 0:DH], in0=y_buf[:, 0:DH],
                in1=psREP[:, 0:1].broadcast_to([P, DH]), op=ALU.subtract),
                waits=[("t", t_rep), ("q", S.cnt["q"])])
            S.sep("v")
            S.rec("v", lambda e: e.tensor_tensor(
                out=x1_buf[:, 0:DH], in0=x1_buf[:, 0:DH],
                in1=psB[:, 0:1].broadcast_to([P, DH]), op=ALU.mult), inc="v")
            v_x1t = S.cnt["v"]
            S.rec("s", lambda e: e.dma_start(
                out=x1_v[:, :, TL:TL + 1],
                in_=x1_buf[:, 0:DH].rearrange("p (k c) -> p k c", k=DH),
                allow_slow_non_contiguous=True),
                waits=[("v", v_x1t)], inc="q")
            q_ph1_done = S.cnt["q"]

        with contextlib.ExitStack() as ph2:
            w1q_s = sb("w1q_s", [P, DH * HID], BF16, ph2)
            w2q_b = [sb("w2q_b0", [P, HH * P], BF16, ph2),
                     sb("w2q_b1", [P, HH * P], BF16, ph2)]
            h1s = sb("h1s", [P, HH * T2], F32, ph2)
            h1q_v = h1s.ap().bitcast(BF16)[:, 0:HH * T2]
            x1c = sb("x1c", [P, DH * (T2 + 2)], F32, ph2)
            x1q = sb("x1q", [P, DH * (T2 + 2)], BF16, ph2)
            u2b = sb("u2b", [P, DH * T2], F32, ph2)
            qrA = sb("qrA", [1, T2 + 2], F32, ph2)
            qrB = sb("qrB", [1, T2 + 2], F32, ph2)
            qrC = sb("qrC", [1, T2 + 2], F32, ph2)
            qrD = sb("qrD", [1, T2 + 2], F32, ph2)
            ab_scr = sb("ab_scr", [P, T2 + 2], F32, ph2)
            cvt = sb("cvt", [P, T2 + 2], F32, ph2)
            scr2 = sb("scr2", [P, DH * T2], F32, ph2)
            rows = sb("rows", [1, 2048], F32, ph2)
            fsb = sb("fsb", [P, T2 + 2], F32, ph2)
            psA = pt("psA", [P, 512], ph2)
            psBk = pt("psBk", [P, 512], ph2)
            psC = pt("psC", [P, 512], ph2)
            psD = pt("psD", [1, 512], ph2)
            psE = pt("psE", [1, 512], ph2)

            x1c_v = x1c.ap().rearrange("p (k c2) -> p k c2", k=DH)
            x1q_3 = x1q.ap().rearrange("p (k c2) -> p k c2", k=DH)
            u2_3 = u2b.ap().rearrange("p (k c2) -> p k c2", k=DH)
            scr2_3 = scr2.ap().rearrange("p (k c2) -> p k c2", k=DH)
            w1v = w1q_s.ap().rearrange("p (k h) -> p k h", k=DH)
            w2sc_v = w2qsc.rearrange("p (d k c2) -> p d k c2", d=DH, k=HH)

            S.rec("s", lambda e: e.dma_start(out=w1q_s[:], in_=w1qsc),
                  waits=S.barrier_w() + [("q", q_wq_done)], inc="q")
            q_w1 = S.cnt["q"]
            if dbg:
                S.rec("s", lambda e: e.dma_start(out=dbgf_w1q, in_=w1q_s[:]),
                      waits=[("q", q_w1)], inc="q")

            prev_store_v = S.cnt["v"]
            for c in range(NCH2):
                o0 = c * T2
                O = min(T2, T - o0)
                padL = (c == 0)
                padR = (o0 + O == T)
                lo = max(0, o0 - 1)
                W = min(T, o0 + O + 1) - lo
                cs = 1 if padL else 0
                WB = O + 2

                S.rec("s", (lambda lo, W, cs: lambda e: e.dma_start(
                    out=x1c_v[:, :, cs:cs + W],
                    in_=x1_v[:, :, lo:lo + W]))(lo, W, cs),
                    waits=[("v", prev_store_v), ("q", q_ph1_done)], inc="q")
                q_x1 = S.cnt["q"]
                if padL:
                    S.rec("v", lambda e: e.memset(x1c_v[:, :, 0:1], 0.0))
                if padR:
                    S.rec("v", (lambda WB: lambda e: e.memset(
                        x1c_v[:, :, WB - 1:WB], 0.0))(WB))
                S.rec("v", lambda e: e.engine_nop(), waits=[("q", q_x1)], inc="v")
                v_x1r = S.cnt["v"]

                for dd in range(DH):
                    S.rec("a", (lambda dd, WB: lambda e: e.activation(
                        out=ab_scr[:, 0:WB], in_=x1c_v[:, dd, 0:WB],
                        func=AF.Abs))(dd, WB),
                        waits=[("v", v_x1r), ("p", S.cnt["p"])],
                        inc="a")
                    a_ab = S.cnt["a"]
                    rb = qrA if dd == 0 else (qrB if dd % 2 else qrD)
                    S.rec("p", (lambda rb, WB: lambda e: e.tensor_reduce(
                        out=rb[0:1, 0:WB], in_=ab_scr[:, 0:WB],
                        axis=AX.C, op=ALU.max))(rb, WB),
                        waits=[("a", a_ab)] + ([("v", S.cnt["v"])] if dd >= 2 else []),
                        inc="p")
                    if dd:
                        S.sep("v")
                        S.rec("v", (lambda rb, WB: lambda e: e.tensor_tensor(
                            out=qrA[0:1, 0:WB], in0=qrA[0:1, 0:WB],
                            in1=rb[0:1, 0:WB], op=ALU.max))(rb, WB),
                            waits=S.w("p"), inc="v")
                p_amax = S.cnt["p"]
                S.sep("v")
                S.rec("v", (lambda WB: lambda e: e.tensor_scalar(
                    out=rows[0:1, 0:WB], in0=qrA[0:1, 0:WB], scalar1=1e-5,
                    scalar2=None, op0=ALU.max))(WB))
                S.sep("v")
                S.rec("v", (lambda WB: lambda e: e.reciprocal(
                    out=rows[0:1, 512:512 + WB], in_=rows[0:1, 0:WB]))(WB))
                S.sep("v")
                S.rec("v", (lambda WB: lambda e: e.tensor_scalar(
                    out=rows[0:1, 512:512 + WB], in0=rows[0:1, 512:512 + WB],
                    scalar1=127.0, scalar2=None, op0=ALU.mult))(WB))
                S.rec("v", (lambda WB: lambda e: e.tensor_scalar(
                    out=rows[0:1, 1024:1024 + WB], in0=rows[0:1, 0:WB],
                    scalar1=consts[0:1, 0:1], scalar2=None, op0=ALU.mult))(WB),
                    inc="v")
                v_rows = S.cnt["v"]
                S.rec("t", (lambda WB: lambda e: nc.tensor.matmul(
                    psA[:, 0:WB], ones1[:], rows[0:1, 512:512 + WB],
                    start=True, stop=True))(WB), waits=[("v", v_rows)], inc="t")
                S.rec("t", (lambda WB: lambda e: nc.tensor.matmul(
                    psBk[:, 0:WB], ones1[:], rows[0:1, 1024:1024 + WB],
                    start=True, stop=True))(WB), inc="t")
                t_reps = S.cnt["t"]
                S.rec("v", (lambda WB: lambda e: e.tensor_scalar(
                    out=fsb[:, 0:WB], in0=psBk[:, 0:WB], scalar1=1.0, scalar2=None,
                    op0=ALU.mult))(WB), waits=[("t", t_reps)], inc="v")
                def q1mul(dd, WB):
                    qb = cvt if dd % 2 == 0 else ab_scr
                    S.rec("v", (lambda dd, WB, qb: lambda e: e.tensor_tensor(
                        out=qb[:, 0:WB], in0=x1c_v[:, dd, 0:WB],
                        in1=psA[:, 0:WB], op=ALU.mult))(dd, WB, qb))

                def q1mag(dd, WB, last):
                    qb = cvt if dd % 2 == 0 else ab_scr
                    S.rec("v", (lambda dd, WB, qb: lambda e: e.tensor_scalar(
                        out=x1q_3[:, dd, 0:WB], in0=qb[:, 0:WB],
                        scalar1=MAGIC, scalar2=MAGIC,
                        op0=ALU.add, op1=ALU.subtract))(dd, WB, qb),
                        inc="v" if last else None)

                q1mul(0, WB)
                for dd in range(1, DH):
                    q1mul(dd, WB)
                    q1mag(dd - 1, WB, False)
                S.sep("v")
                q1mag(DH - 1, WB, True)
                v_x1q = S.cnt["v"]

                S.rec("v", lambda e: e.memset(qrC[0:1, 0:T2], 0.0), inc="v")
                for h in range(HH):
                    pb = psC if h % 2 == 0 else psA
                    for k in range(DH):
                        S.rec("t", (lambda h, k, pb, WB: lambda e: nc.tensor.matmul(
                            pb[:, 0:WB], w1v[:, k, h * P:(h + 1) * P],
                            x1q_3[:, k, 0:WB],
                            start=(k == 0), stop=(k == DH - 1)))(h, k, pb, WB),
                            waits=([("v", v_x1q), ("q", q_w1)] if h <= 1
                                   else [("v", S.cnt["v"])]) if k == 0 else (),
                            inc="t" if k == DH - 1 else None)
                    t_h = S.cnt["t"]
                    S.rec("v", (lambda pb, WB: lambda e: e.tensor_tensor(
                        out=scr2[:, 0:WB], in0=pb[:, 0:WB], in1=fsb[:, 0:WB],
                        op=ALU.mult))(pb, WB),
                        waits=[("t", t_h), ("a", S.cnt["a"])])
                    S.sep("v")
                    S.rec("v", (lambda h, O: lambda e: e.tensor_scalar(
                        out=cvt[:, 0:O], in0=scr2[:, 1:1 + O],
                        scalar1=convw_s[:, h * 3 + 1:h * 3 + 2], scalar2=None,
                        op0=ALU.mult))(h, O))
                    S.sep("v")
                    S.rec("v", (lambda h, O: lambda e: e.scalar_tensor_tensor(
                        out=cvt[:, 0:O], in0=scr2[:, 0:O],
                        scalar=convw_s[:, h * 3:h * 3 + 1], in1=cvt[:, 0:O],
                        op0=ALU.mult, op1=ALU.add))(h, O))
                    S.sep("v")
                    S.rec("v", (lambda h, O: lambda e: e.scalar_tensor_tensor(
                        out=cvt[:, 0:O], in0=scr2[:, 2:2 + O],
                        scalar=convw_s[:, h * 3 + 2:h * 3 + 3], in1=cvt[:, 0:O],
                        op0=ALU.mult, op1=ALU.add))(h, O), inc="v")
                    v_cv = S.cnt["v"]
                    S.rec("a", (lambda h, O: lambda e: e.activation(
                        out=h1s[:, h * T2:h * T2 + O], in_=cvt[:, 0:O],
                        func=AF.Silu, bias=convb_s[:, h:h + 1]))(h, O),
                        waits=[("v", v_cv)], inc="a")
                    a_h = S.cnt["a"]
                    S.sep("a")
                    S.rec("a", (lambda h, O: lambda e: e.activation(
                        out=ab_scr[:, 0:O], in_=h1s[:, h * T2:h * T2 + O],
                        func=AF.Abs))(h, O), waits=S.w("p"), inc="a")
                    a_ab2 = S.cnt["a"]
                    rb = qrB if h % 2 else qrD
                    S.rec("p", (lambda rb, h, O: lambda e: e.tensor_reduce(
                        out=rb[0:1, 0:O], in_=ab_scr[:, 0:O],
                        axis=AX.C, op=ALU.max))(rb, h, O),
                        waits=[("a", a_ab2)] + ([("v", S.cnt["v"])] if h >= 2 else []),
                        inc="p")
                    S.rec("v", (lambda rb, O: lambda e: e.tensor_tensor(
                        out=qrC[0:1, 0:O], in0=qrC[0:1, 0:O], in1=rb[0:1, 0:O],
                        op=ALU.max))(rb, O), waits=S.w("p"), inc="v")
                p_am2 = S.cnt["p"]
                if dbg and c == 0:
                    S.rec("s", lambda e: e.dma_start(out=dbgf_h1s, in_=h1s[:]),
                          waits=S.w("v", "a"), inc="q")
                    S.rec("v", lambda e: e.engine_nop(), waits=S.w("q"), inc="v")
                S.sep("v")
                S.rec("v", (lambda O: lambda e: e.tensor_scalar(
                    out=rows[0:1, 0:O], in0=qrC[0:1, 0:O], scalar1=1e-5,
                    scalar2=None, op0=ALU.max))(O))
                S.sep("v")
                S.rec("v", (lambda O: lambda e: e.reciprocal(
                    out=rows[0:1, 512:512 + O], in_=rows[0:1, 0:O]))(O))
                S.sep("v")
                S.rec("v", (lambda O: lambda e: e.tensor_scalar(
                    out=rows[0:1, 512:512 + O], in0=rows[0:1, 512:512 + O],
                    scalar1=127.0, scalar2=None, op0=ALU.mult))(O))
                S.rec("v", (lambda O: lambda e: e.tensor_scalar(
                    out=rows[0:1, 1024:1024 + O], in0=rows[0:1, 0:O],
                    scalar1=consts[0:1, 1:2], scalar2=None, op0=ALU.mult))(O),
                    inc="v")
                v_r2 = S.cnt["v"]
                S.rec("t", (lambda O: lambda e: nc.tensor.matmul(
                    psA[:, 0:O], ones1[:], rows[0:1, 512:512 + O],
                    start=True, stop=True))(O), waits=[("v", v_r2)], inc="t")
                S.rec("t", (lambda O: lambda e: nc.tensor.matmul(
                    psBk[:, 0:O], ones1[:], rows[0:1, 1024:1024 + O],
                    start=True, stop=True))(O), inc="t")
                t_r2 = S.cnt["t"]
                S.rec("v", (lambda O: lambda e: e.tensor_scalar(
                    out=fsb[:, 0:O], in0=psBk[:, 0:O], scalar1=1.0, scalar2=None,
                    op0=ALU.mult))(O), waits=[("t", t_r2)], inc="v")
                def q2mul(h, O):
                    qb = cvt if h % 2 == 0 else ab_scr
                    S.rec("v", (lambda h, O, qb: lambda e: e.tensor_tensor(
                        out=qb[:, 0:O], in0=h1s[:, h * T2:h * T2 + O],
                        in1=psA[:, 0:O], op=ALU.mult))(h, O, qb))

                def q2mag(h, O, last):
                    qb = cvt if h % 2 == 0 else ab_scr
                    S.rec("v", (lambda h, O, qb: lambda e: e.tensor_scalar(
                        out=h1q_v[:, h * T2:h * T2 + O], in0=qb[:, 0:O],
                        scalar1=MAGIC, scalar2=MAGIC,
                        op0=ALU.add, op1=ALU.subtract))(h, O, qb),
                        inc="v" if last else None)

                q2mul(0, O)
                for h in range(1, HH):
                    q2mul(h, O)
                    q2mag(h - 1, O, False)
                S.sep("v")
                q2mag(HH - 1, O, True)
                v_h1q = S.cnt["v"]
                if dbg and c == 0:
                    S.rec("s", lambda e: e.dma_start(out=dbgf_h1q, in_=h1q_v),
                          waits=[("v", v_h1q)], inc="q")
                    S.rec("v", lambda e: e.engine_nop(), waits=S.w("q"), inc="v")

                for dd in range(DH):
                    S.rec("s", (lambda dd: lambda e: e.dma_start(
                        out=w2q_b[dd % 2][:], in_=w2sc_v[:, dd]))(dd),
                        waits=[("t", S.cnt["t"]), ("q", q_wq_done)], inc="q")
                    q_w2 = S.cnt["q"]
                    for k in range(HH):
                        S.rec("t", (lambda dd, k, O: lambda e: nc.tensor.matmul(
                            psC[:, 0:O], w2q_b[dd % 2][:, k * P:(k + 1) * P],
                            h1q_v[:, k * T2:k * T2 + O],
                            start=(k == 0), stop=(k == HH - 1)))(dd, k, O),
                            waits=[("q", q_w2), ("v", v_h1q)] if k == 0 else (),
                            inc="t" if k == HH - 1 else None)
                    t_m2 = S.cnt["t"]
                    S.rec("v", (lambda dd, O: lambda e: e.tensor_tensor(
                        out=u2_3[:, dd, 0:O], in0=psC[:, 0:O], in1=fsb[:, 0:O],
                        op=ALU.mult))(dd, O),
                        waits=[("t", t_m2)] + ([("q", S.cnt["q"])] if dd == 0 else []))
                    S.sep("v")
                    S.rec("v", (lambda dd, O: lambda e: e.tensor_tensor(
                        out=u2_3[:, dd, 0:O], in0=u2_3[:, dd, 0:O],
                        in1=x1c_v[:, dd, 1:1 + O], op=ALU.add))(dd, O), inc="v")
                v_u2 = S.cnt["v"]
                S.rec("a", (lambda O: lambda e: e.activation(
                    out=scr2_3[:, :, 0:O], in_=u2_3[:, :, 0:O], func=AF.Square))(O),
                    waits=[("v", v_u2)], inc="a")
                a_sq2 = S.cnt["a"]
                for dd in range(DH):
                    S.rec("t", (lambda dd, O: lambda e: nc.tensor.matmul(
                        psD[0:1, 0:O], ones128[:], u2_3[:, dd, 0:O],
                        start=(dd == 0), stop=(dd == DH - 1)))(dd, O),
                        waits=[("v", v_u2)] if dd == 0 else (),
                        inc="t" if dd == DH - 1 else None)
                for dd in range(DH):
                    S.rec("t", (lambda dd, O: lambda e: nc.tensor.matmul(
                        psE[0:1, 0:O], ones128[:], scr2_3[:, dd, 0:O],
                        start=(dd == 0), stop=(dd == DH - 1)))(dd, O),
                        waits=[("a", a_sq2)] if dd == 0 else (),
                        inc="t" if dd == DH - 1 else None)
                t_ln2 = S.cnt["t"]
                S.rec("v", (lambda O: lambda e: e.tensor_scalar(
                    out=rows[0:1, 0:O], in0=psD[0:1, 0:O], scalar1=1.0 / D,
                    scalar2=None, op0=ALU.mult))(O), waits=[("t", t_ln2)])
                S.rec("v", (lambda O: lambda e: e.tensor_scalar(
                    out=rows[0:1, 512:512 + O], in0=psE[0:1, 0:O], scalar1=1.0 / D,
                    scalar2=None, op0=ALU.mult))(O))
                S.rec("v", (lambda O: lambda e: e.tensor_tensor(
                    out=rows[0:1, 1024:1024 + O], in0=rows[0:1, 0:O],
                    in1=rows[0:1, 0:O], op=ALU.mult))(O))
                S.sep("v")
                S.rec("v", (lambda O: lambda e: e.tensor_tensor(
                    out=rows[0:1, 512:512 + O], in0=rows[0:1, 512:512 + O],
                    in1=rows[0:1, 1024:1024 + O], op=ALU.subtract))(O), inc="v")
                S.rec("a", (lambda O: lambda e: e.activation(
                    out=rows[0:1, 1024:1024 + O], in_=rows[0:1, 512:512 + O],
                    func=AF.Ln, bias=cb[0:1, 2:3]))(O), waits=S.w("v"))
                S.sep("a")
                S.rec("a", (lambda O: lambda e: e.activation(
                    out=rows[0:1, 1536:1536 + O], in_=rows[0:1, 1024:1024 + O],
                    func=AF.Exp, scale=-0.5))(O), inc="a")
                S.rec("t", (lambda O: lambda e: nc.tensor.matmul(
                    psA[:, 0:O], ones1[:], rows[0:1, 0:O],
                    start=True, stop=True))(O), waits=S.w("a"), inc="t")
                S.rec("t", (lambda O: lambda e: nc.tensor.matmul(
                    psBk[:, 0:O], ones1[:], rows[0:1, 1536:1536 + O],
                    start=True, stop=True))(O), inc="t")
                t_rp2 = S.cnt["t"]
                S.rec("v", (lambda O: lambda e: e.tensor_tensor(
                    out=u2_3[:, :, 0:O], in0=u2_3[:, :, 0:O],
                    in1=psA[:, 0:O].unsqueeze(1).broadcast_to([P, DH, O]),
                    op=ALU.subtract))(O), waits=[("t", t_rp2)])
                S.sep("v")
                S.rec("v", (lambda O: lambda e: e.tensor_tensor(
                    out=u2_3[:, :, 0:O], in0=u2_3[:, :, 0:O],
                    in1=psBk[:, 0:O].unsqueeze(1).broadcast_to([P, DH, O]),
                    op=ALU.mult))(O), inc="v")
                prev_store_v = S.cnt["v"]
                S.rec("s", (lambda o0, O: lambda e: e.dma_start(
                    out=out_v[:, :, o0:o0 + O],
                    in_=u2_3[:, :, 0:O]))(o0, O),
                    waits=[("v", prev_store_v)], inc="q")

            q_final = S.cnt["q"]

            with nc.Block() as block:
                @block.sync
                def _(e):
                    S.emit("s", e, sems)
                    e.wait_ge(sems["q"], q_final)

                @block.vector
                def _(e):
                    S.emit("v", e, sems)
                    e.wait_ge(sems["q"], q_final)

                @block.scalar
                def _(e):
                    S.emit("a", e, sems)

                @block.tensor
                def _(e):
                    S.emit("t", e, sems)

                @block.gpsimd
                def _(e):
                    S.emit("p", e, sems)

    return nc


_NC_CACHE = None


def kernel(L0_attn_x, L0_attn_W_dt, L0_attn_b_dt, L0_attn_W_B, L0_attn_W_C,
           L0_attn_A_log, L0_attn_D_param, L0_ffn_w1, L0_ffn_conv_w,
           L0_ffn_conv_b, L0_ffn_w2):
    global _NC_CACHE
    B = np.asarray(L0_attn_x).shape[0]
    assert B == 8

    def fm(a, cols):
        a = np.asarray(a)
        R = a.shape[0]
        return np.ascontiguousarray(
            a.reshape(R // P, P, cols).transpose(1, 0, 2).reshape(P, (R // P) * cols)
        ).astype(np.float32)

    wdtT_h = fm(np.asarray(L0_attn_W_dt).T, D)
    wbc = np.concatenate([np.asarray(L0_attn_W_B), np.asarray(L0_attn_W_C)], axis=0)
    wbcT_h = fm(np.ascontiguousarray(wbc.T), 32)
    alog_h = fm(np.asarray(L0_attn_A_log), N)
    bdt_h = np.ascontiguousarray(np.asarray(L0_attn_b_dt).reshape(DH, P).T).astype(np.float32)
    dpar_h = np.ascontiguousarray(np.asarray(L0_attn_D_param).reshape(DH, P).T).astype(np.float32)
    convw_h = fm(np.asarray(L0_ffn_conv_w)[:, 0, :], 3)
    convb_h = np.ascontiguousarray(np.asarray(L0_ffn_conv_b).reshape(HH, P).T).astype(np.float32)
    w1T_h = fm(np.ascontiguousarray(np.asarray(L0_ffn_w1).T), HID)
    w2T_h = np.ascontiguousarray(
        np.asarray(L0_ffn_w2).reshape(DH, P, HH, P).transpose(3, 0, 2, 1)
        .reshape(P, DH * HID)).astype(np.float32)
    idn_h = np.eye(P, dtype=np.float32)

    if _NC_CACHE is None:
        _NC_CACHE = build_program()
    nc = _NC_CACHE

    in_maps = []
    for b in range(B):
        xt = np.ascontiguousarray(np.asarray(L0_attn_x[b]).T).astype(np.float32)
        in_maps.append(dict(x_t=xt, wdtT=wdtT_h, wbcT=wbcT_h, alog=alog_h,
                            bdt=bdt_h, dpar=dpar_h, convw=convw_h, convb=convb_h,
                            idn=idn_h, w1T=w1T_h, w2T=w2T_h))
    res = run_bass_kernel_spmd(nc, in_maps, core_ids=list(range(B)))
    out = np.empty((B, T, D), dtype=np.float32)
    for b in range(B):
        out[b] = np.asarray(res.results[b]["out_t"]).T
    return out


# version 08
# speedup vs baseline: 1.9779x; 1.9779x over previous
import contextlib
import math
import os
import numpy as np
import concourse.bass as bass
import concourse.mybir as mybir
from concourse.bass_utils import run_bass_kernel_spmd

F32 = mybir.dt.float32
BF16 = mybir.dt.bfloat16
AF = mybir.ActivationFunctionType
ALU = mybir.AluOpType
AX = mybir.AxisListType

P = 128
D = 1024
DH = 8
T = 8192
N = 16
TB = 90
NSEG = 9
NCH = 91
HID = 4096
HH = 32
T2 = 448
NCH2 = (T + T2 - 1) // T2

MAGIC = 12582912.0
SC_DT_SCALE = 0.5 / (0.0495 + 1e-8)
SC_DT_BIAS = -0.0505 * SC_DT_SCALE
SC_DT_RNG = 0.099
SC_DT_CTR = 0.0505
SC8_SCALE = 0.5 / (8.0 + 1e-8)
SC5_SCALE = 0.5 / (5.0 + 1e-8)


class Sched:
    def __init__(self):
        self.ops = {e: [] for e in ("v", "a", "t", "p", "s")}
        self.cnt = {e: 0 for e in ("v", "a", "t", "p", "q")}

    def rec(self, eng, fn, waits=(), inc=None):
        self.ops[eng].append((fn, list(waits), inc))
        if inc is not None:
            self.cnt[inc] += 16 if inc == "q" else 1

    def w(self, *engs):
        return [(e, self.cnt[e]) for e in engs]

    def sep(self, eng):
        if eng == "a":
            self.rec("a", lambda e: e.nop())
        elif eng == "t":
            self.rec("t", lambda e: e.nop())
        else:
            self.rec(eng, lambda e: e.engine_nop())

    def barrier_w(self):
        return [(e, self.cnt[e]) for e in ("v", "a", "t", "p")]

    def emit(self, eng_name, e, sems):
        last_wait = {}
        for fn, waits, inc in self.ops[eng_name]:
            for sem, val in waits:
                if val > 0 and last_wait.get(sem, -1) < val:
                    e.wait_ge(sems[sem], val)
                    last_wait[sem] = val
            ret = fn(e)
            if inc is not None:
                ret.then_inc(sems[inc], 16 if inc == "q" else 1)


def build_program(dbg=False):
    nc = bass.Bass("TRN2", target_bir_lowering=False,
                   detect_race_conditions=False)

    x_t = nc.dram_tensor("x_t", [D, T], F32, kind="ExternalInput").ap()
    wdtT = nc.dram_tensor("wdtT", [P, DH * D], F32, kind="ExternalInput").ap()
    wbcT = nc.dram_tensor("wbcT", [P, DH * 32], F32, kind="ExternalInput").ap()
    alog = nc.dram_tensor("alog", [P, DH * N], F32, kind="ExternalInput").ap()
    bdt_in = nc.dram_tensor("bdt", [P, DH], F32, kind="ExternalInput").ap()
    dpar_in = nc.dram_tensor("dpar", [P, DH], F32, kind="ExternalInput").ap()
    convw_in = nc.dram_tensor("convw", [P, HH * 3], F32, kind="ExternalInput").ap()
    convb_in = nc.dram_tensor("convb", [P, HH], F32, kind="ExternalInput").ap()
    idn_in = nc.dram_tensor("idn", [P, P], F32, kind="ExternalInput").ap()
    w1T_in = nc.dram_tensor("w1T", [P, DH * HID], F32, kind="ExternalInput").ap()
    w2T_in = nc.dram_tensor("w2T", [P, HH * D], F32, kind="ExternalInput").ap()
    out_t = nc.dram_tensor("out_t", [D, T], F32, kind="ExternalOutput").ap()
    x1sc = nc.dram_tensor("x1sc", [D, T], F32, kind="ExternalOutput" if dbg else "Internal").ap()
    if dbg:
        dbg_dt = nc.dram_tensor("dbg_dt", [P, DH * 91], F32, kind="ExternalOutput").ap()
        dbg_a = nc.dram_tensor("dbg_a", [P, NSEG * 1280], F32, kind="ExternalOutput").ap()
        dbg_ast = nc.dram_tensor("dbg_ast", [P, NSEG * P], F32, kind="ExternalOutput").ap()
        dbg_uh = nc.dram_tensor("dbg_uh", [P, NSEG * 1280], F32, kind="ExternalOutput").ap()
        dbg_y = nc.dram_tensor("dbg_y", [P, DH * 91], F32, kind="ExternalOutput").ap()
        dbg_bp = nc.dram_tensor("dbg_bp", [P, N * 91], F32, kind="ExternalOutput").ap()
        dbg_cp = nc.dram_tensor("dbg_cp", [P, N * 91], F32, kind="ExternalOutput").ap()
        dbg_carry = nc.dram_tensor("dbg_carry", [P, P], F32, kind="ExternalOutput").ap()
        dbg_c0 = nc.dram_tensor("dbg_c0", [P, P], F32, kind="ExternalOutput").ap()
        dbg_u = nc.dram_tensor("dbg_u", [P, NSEG * 1280], F32, kind="ExternalOutput").ap()
        dbg_scl = nc.dram_tensor("dbg_scl", [P, 16], F32, kind="ExternalOutput").ap()
        dbgf_w1q = nc.dram_tensor("dbgf_w1q", [P, DH * HID], BF16, kind="ExternalOutput").ap()
        dbgf_x1q = nc.dram_tensor("dbgf_x1q", [P, DH * (T2 + 2)], BF16, kind="ExternalOutput").ap()
        dbgf_h1s = nc.dram_tensor("dbgf_h1s", [P, HH * T2], F32, kind="ExternalOutput").ap()
        dbgf_h1q = nc.dram_tensor("dbgf_h1q", [P, HH * T2], BF16, kind="ExternalOutput").ap()
        dbgf_rows = nc.dram_tensor("dbgf_rows", [1, 2048], F32, kind="ExternalOutput").ap()
        dbgf_fsb = nc.dram_tensor("dbgf_fsb", [P, T2 + 2], F32, kind="ExternalOutput").ap()
        dbgf_consts = nc.dram_tensor("dbgf_consts", [1, 16], F32, kind="ExternalOutput").ap()
    w1qsc = nc.dram_tensor("w1qsc", [P, DH * HID], BF16, kind="Internal").ap()
    w2qsc = nc.dram_tensor("w2qsc", [P, DH * HID], BF16, kind="Internal").ap()

    x_v = x_t.rearrange("(k p) t -> p k t", p=P)
    x1_v = x1sc.rearrange("(k p) t -> p k t", p=P)
    out_v = out_t.rearrange("(k p) t -> p k t", p=P)

    S = Sched()
    with contextlib.ExitStack() as top:
        sems = {n: top.enter_context(nc.semaphore("sem_" + n))
                for n in ("v", "a", "t", "p", "q")}

        def sb(name, shape, dt=F32, stack=top):
            return stack.enter_context(nc.sbuf_tensor(name, shape, dt))

        def pt(name, shape, stack=top):
            return stack.enter_context(nc.psum_tensor(name, shape, F32))

        consts = sb("consts", [1, 16])
        ones128 = sb("ones128", [P, 1])
        ones1 = sb("ones1", [1, P])
        A_res = sb("A_res", [P, DH * N])
        bdt_s = sb("bdt_s", [P, DH])
        dp1_s = sb("dp1_s", [P, DH])
        convw_s = sb("convw_s", [P, HH * 3])
        convb_s = sb("convb_s", [P, HH])
        idn_s = sb("idn_s", [P, P])
        cb = sb("cb", [P, 8])

        with contextlib.ExitStack() as ph0:
            wbuf = sb("wbuf", [P, DH * HID], F32, ph0)
            wq = wbuf.ap().bitcast(BF16)[:, 0:DH * HID]
            accs = sb("accs", [P, 8], F32, ph0)
            asc = sb("asc", [P, 4096], F32, ph0)
            rowt = sb("rowt", [1, 8], F32, ph0)
            ps0 = pt("ps0", [1, 512], ph0)
            psb0 = pt("psb0", [P, 512], ph0)

            for dst, src in ((A_res, alog), (bdt_s, bdt_in), (dp1_s, dpar_in),
                             (convw_s, convw_in), (convb_s, convb_in),
                             (idn_s, idn_in)):
                S.rec("s", (lambda d, sr: lambda e: e.dma_start(out=d[:], in_=sr))(dst, src),
                      inc="q")
            S.rec("v", lambda e: e.memset(ones128[:], 1.0))
            S.rec("v", lambda e: e.memset(ones1[:], 1.0))
            S.rec("v", lambda e: e.memset(cb[:, 0:1], SC_DT_BIAS))
            S.rec("v", lambda e: e.memset(cb[:, 1:2], 1.0))
            S.rec("v", lambda e: e.memset(cb[:, 2:3], 1e-5))
            S.rec("v", lambda e: e.memset(cb[:, 3:4], math.log(20.0)))
            S.rec("v", lambda e: e.memset(consts[:], 0.0), inc="v")
            v_consts = S.cnt["v"]
            S.rec("a", lambda e: e.activation(out=asc[:, 0:DH * N], in_=A_res[:],
                                              func=AF.Tanh, scale=SC5_SCALE),
                  waits=S.w("q") + [("v", v_consts)], inc="a")
            S.sep("a")
            S.rec("a", lambda e: e.activation(out=A_res[:], in_=asc[:, 0:DH * N],
                                              func=AF.Exp, scale=10.0), inc="a")
            S.rec("v", lambda e: e.tensor_scalar(out=A_res[:], in0=A_res[:],
                                                 scalar1=-1.0, scalar2=None,
                                                 op0=ALU.mult),
                  waits=S.w("a"), inc="v")
            S.rec("v", lambda e: e.tensor_scalar(out=dp1_s[:], in0=dp1_s[:],
                                                 scalar1=1.0, scalar2=None,
                                                 op0=ALU.add), inc="v")

            for wi, (wsrc, wdst, cslot) in enumerate(
                    ((w1T_in, w1qsc, 0), (w2T_in, w2qsc, 1))):
                S.rec("s", (lambda ws: lambda e: e.dma_start(out=wbuf[:], in_=ws))(wsrc),
                      waits=S.w("v"), inc="q")
                qw = S.cnt["q"]
                for sl in range(8):
                    S.rec("a", (lambda sl: lambda e: e.activation(
                        out=asc[:], in_=wbuf[:, sl * 4096:(sl + 1) * 4096],
                        func=AF.Abs, accum_out=accs[:, sl:sl + 1]))(sl),
                        waits=[("q", qw)], inc="a")
                S.rec("v", lambda e: e.tensor_reduce(out=accs[:, 0:1], in_=accs[:, 0:8],
                                                     axis=AX.X, op=ALU.add),
                      waits=S.w("a"), inc="v")
                S.rec("t", lambda e: nc.tensor.matmul(ps0[0:1, 0:1], ones128[:],
                                                      accs[:, 0:1], start=True, stop=True),
                      waits=S.w("v"), inc="t")
                S.rec("v", lambda e: e.tensor_scalar(out=rowt[0:1, 0:1], in0=ps0[0:1, 0:1],
                                                     scalar1=1.0 / (HID * D), scalar2=1e-5,
                                                     op0=ALU.mult, op1=ALU.max),
                      waits=S.w("t"), inc="v")
                S.sep("v")
                S.rec("v", (lambda cs: lambda e: e.tensor_scalar(
                    out=consts[0:1, cs:cs + 1], in0=rowt[0:1, 0:1],
                    scalar1=1.0 / 127.0, scalar2=None, op0=ALU.mult))(cslot), inc="v")
                S.rec("v", lambda e: e.reciprocal(out=rowt[0:1, 1:2], in_=rowt[0:1, 0:1]),
                      inc="v")
                S.rec("t", lambda e: nc.tensor.matmul(psb0[:, 0:1], ones1[:],
                                                      rowt[0:1, 1:2], start=True, stop=True),
                      waits=S.w("v"), inc="t")
                qprev = S.cnt["q"]
                for sl in range(8):
                    S.rec("v", (lambda sl: lambda e: e.tensor_scalar(
                        out=asc[:], in0=wbuf[:, sl * 4096:(sl + 1) * 4096],
                        scalar1=psb0[:, 0:1], scalar2=None, op0=ALU.mult))(sl),
                        waits=S.w("t") + [("q", qprev)])
                    S.sep("v")
                    S.rec("v", lambda e: e.tensor_scalar(
                        out=asc[:], in0=asc[:], scalar1=MAGIC, scalar2=MAGIC,
                        op0=ALU.add, op1=ALU.subtract))
                    S.sep("v")
                    S.rec("v", (lambda sl: lambda e: e.tensor_scalar(
                        out=wq[:, sl * 4096:(sl + 1) * 4096], in0=asc[:],
                        scalar1=1.0, scalar2=-1.0, op0=ALU.min, op1=ALU.max))(sl),
                        inc="v")
                S.rec("s", (lambda wd: lambda e: e.dma_start(out=wd, in_=wq))(wdst),
                      waits=S.w("v"), inc="q")
            q_wq_done = S.cnt["q"]

        with contextlib.ExitStack() as ph1:
            wdt_s = sb("wdt_s", [P, DH * D], F32, ph1)
            wbc_s = sb("wbc_s", [P, DH * 32], F32, ph1)
            a_buf = sb("a_buf", [P, NSEG * 1280], F32, ph1)
            uh_buf = sb("uh_buf", [P, NSEG * 1280], F32, ph1)
            scr = sb("scr", [P, 3840], F32, ph1)
            x_buf = sb("x_buf", [P, DH * 91], F32, ph1)
            r_buf = sb("r_buf", [P, DH * 91], F32, ph1)
            va_buf = sb("va_buf", [P, DH * 91], F32, ph1)
            e_buf = sb("e_buf", [P, DH * 91], F32, ph1)
            dt_buf = sb("dt_buf", [P, DH * 91], F32, ph1)
            z_buf = sb("z_buf", [P, DH * 91], F32, ph1)
            ast_buf = sb("ast_buf", [P, NSEG * P], F32, ph1)
            bp_rep = sb("bp_rep", [P, N * 91], F32, ph1)
            c_rep = sb("c_rep", [P, N * 91], F32, ph1)
            bp_row = sb("bp_row", [1, N * 91], F32, ph1)
            c_row = sb("c_row", [1, N * 91], F32, ph1)
            y_buf = sb("y_buf", [P, DH * 91], F32, ph1)
            x1_buf = sb("x1_buf", [P, DH * 91], F32, ph1)
            carry = sb("carry", [P, P], F32, ph1)
            itmp = sb("itmp", [P, P], F32, ph1)
            w_tmp = sb("w_tmp", [P, 720], F32, ph1)
            w_tmp2 = sb("w_tmp2", [P, 720], F32, ph1)
            sqs = sb("sqs", [P, 1], F32, ph1)
            sq_scr = sb("sq_scr", [P, P], F32, ph1)
            scl = sb("scl", [1, 8], F32, ph1)
            lrow = sb("lrow", [1, 4 * 91], F32, ph1)
            bc_sb = sb("bc_sb", [32, 91], F32, ph1)

            psDT = [pt("psDT0", [P, 512], ph1), pt("psDT1", [P, 512], ph1)]
            psBC = pt("psBC", [32, 512], ph1)
            psREP = pt("psREP", [P, 512], ph1)
            psN = pt("psN", [1, 512], ph1)
            psB = pt("psB", [P, 512], ph1)
            psLN = pt("psLN", [1, 512], ph1)
            psLN2 = pt("psLN2", [1, 512], ph1)

            wdt_v = wdt_s.ap().rearrange("p (k c) -> p k c", k=DH)
            wbc_v = wbc_s.ap().rearrange("p (k c) -> p k c", k=DH)
            a_4d = a_buf.ap().rearrange("p (k s j) -> p k s j", k=NSEG, s=P)
            uh_4d = uh_buf.ap().rearrange("p (k s j) -> p k s j", k=NSEG, s=P)

            S.rec("s", lambda e: e.dma_start(out=wdt_s[:], in_=wdtT),
                  waits=S.barrier_w(), inc="q")
            S.rec("s", lambda e: e.dma_start(out=wbc_s[:], in_=wbcT), inc="q")
            S.rec("v", lambda e: e.memset(carry[:], 0.0), inc="v")

            def s_chain(hview):
                S.rec("a", (lambda hv: lambda e: e.activation(
                    out=sq_scr[:], in_=hv, func=AF.Square,
                    accum_out=sqs[:, 0:1]))(hview), waits=S.w("v"), inc="a")
                S.rec("t", lambda e: nc.tensor.matmul(psN[0:1, 0:1], ones128[:],
                                                      sqs[:, 0:1], start=True, stop=True),
                      waits=S.w("a"))
                S.rec("t", lambda e: nc.tensor.matmul(psLN[0:1, 0:1], ones128[:],
                                                      sqs[:, 0:1], start=True, stop=True),
                      inc="t")
                S.rec("a", lambda e: e.activation(out=scl[0:1, 0:1], in_=psN[0:1, 0:1],
                                                  func=AF.Ln), waits=S.w("t"), inc="a")
                S.sep("a")
                S.rec("a", lambda e: e.activation(out=scl[0:1, 1:2], in_=scl[0:1, 0:1],
                                                  func=AF.Exp, scale=-0.5,
                                                  bias=cb[0:1, 3:4]))
                S.sep("a")
                S.rec("a", lambda e: e.activation(out=scl[0:1, 2:3], in_=scl[0:1, 1:2],
                                                  func=AF.Relu, scale=-1.0,
                                                  bias=cb[0:1, 1:2]))
                S.sep("a")
                S.rec("a", lambda e: e.activation(out=scl[0:1, 4:5], in_=scl[0:1, 2:3],
                                                  func=AF.Relu, scale=-1.0,
                                                  bias=cb[0:1, 1:2]))
                S.sep("a")
                S.rec("a", lambda e: e.activation(out=scl[0:1, 5:6], in_=scl[0:1, 4:5],
                                                  func=AF.Copy), inc="a")
                S.rec("t", lambda e: nc.tensor.matmul(psB[:, 0:1], ones1[:],
                                                      scl[0:1, 4:5], start=True, stop=True),
                      waits=S.w("a"))
                S.rec("t", lambda e: nc.tensor.matmul(psLN2[0:1, 0:1], ones1[0:1, 0:1],
                                                      scl[0:1, 4:5], start=True, stop=True),
                      inc="t")
                S.sep("v")
                S.rec("v", (lambda hv: lambda e: e.tensor_scalar(
                    out=hv, in0=hv, scalar1=psB[:, 0:1], scalar2=None,
                    op0=ALU.mult))(hview), waits=S.w("t"), inc="v")

            prev_gather_q = 0
            prev_Vx1 = S.cnt["v"]

            for m in range(NCH):
                tok0 = 0 if m == 0 else 90 * m + 1
                Sc = 91 if m == 0 else 90
                xo = 1 if m == 0 else 0
                a_chunk0 = S.cnt["a"]

                S.rec("s", (lambda t0, sc: lambda e: e.dma_start(
                    out=x_buf[:, 0:DH * sc].rearrange("p (k c) -> p k c", k=DH),
                    in_=x_v[:, :, t0:t0 + sc]))(tok0, Sc),
                    waits=[("v", prev_Vx1)], inc="q")
                q_x = S.cnt["q"]

                a_base = S.cnt["a"]
                for dd in range(DH):
                    w0 = [("q", q_x)]
                    if dd >= 2:
                        w0.append(("a", a_base + (dd - 1)))
                    else:
                        w0.append(("a", a_chunk0))
                    for k in range(DH):
                        S.rec("t", (lambda dd, k, sc: lambda e: nc.tensor.matmul(
                            psDT[dd % 2][:, 0:sc], wdt_v[:, k, dd * P:(dd + 1) * P],
                            x_buf[:, 0:DH * sc].rearrange("p (q c) -> p q c", q=DH)[:, k],
                            start=(k == 0), stop=(k == DH - 1)))(dd, k, Sc),
                            waits=w0 if k == 0 else (),
                            inc="t" if k == DH - 1 else None)
                    tq = S.cnt["t"]
                    S.rec("a", (lambda dd, sc: lambda e: e.activation(
                        out=r_buf[:, dd * sc:(dd + 1) * sc], in_=psDT[dd % 2][:, 0:sc],
                        func=AF.Relu, bias=bdt_s[:, dd:dd + 1]))(dd, Sc),
                        waits=[("t", tq)])
                    S.rec("a", (lambda dd, sc: lambda e: e.activation(
                        out=va_buf[:, dd * sc:(dd + 1) * sc], in_=psDT[dd % 2][:, 0:sc],
                        func=AF.Abs, bias=bdt_s[:, dd:dd + 1]))(dd, Sc), inc="a")

                for k in range(DH):
                    S.rec("t", (lambda k, sc: lambda e: nc.tensor.matmul(
                        psBC[0:32, 0:sc], wbc_v[:, k, :],
                        x_buf[:, 0:DH * sc].rearrange("p (q c) -> p q c", q=DH)[:, k],
                        start=(k == 0), stop=(k == DH - 1)))(k, Sc),
                        waits=[("q", max(prev_gather_q, q_x))] if k == 0 else (),
                        inc="t" if k == DH - 1 else None)
                t_bc = S.cnt["t"]

                S.rec("a", (lambda sc: lambda e: e.activation(
                    out=e_buf[:, 0:DH * sc], in_=va_buf[:, 0:DH * sc],
                    func=AF.Exp, scale=-1.0))(Sc))
                S.rec("a", (lambda sc: lambda e: e.activation(
                    out=va_buf[:, 0:DH * sc], in_=e_buf[:, 0:DH * sc],
                    func=AF.Ln, bias=cb[:, 1:2]))(Sc), inc="a")
                S.rec("v", (lambda sc: lambda e: e.tensor_tensor(
                    out=r_buf[:, 0:DH * sc], in0=r_buf[:, 0:DH * sc],
                    in1=va_buf[:, 0:DH * sc], op=ALU.add))(Sc),
                    waits=S.w("a"), inc="v")
                S.rec("a", (lambda sc: lambda e: e.activation(
                    out=e_buf[:, 0:DH * sc], in_=r_buf[:, 0:DH * sc],
                    func=AF.Tanh, scale=SC_DT_SCALE, bias=cb[:, 0:1]))(Sc),
                    waits=S.w("v"), inc="a")
                S.rec("v", (lambda sc: lambda e: e.tensor_scalar(
                    out=dt_buf[:, 0:DH * sc], in0=e_buf[:, 0:DH * sc],
                    scalar1=SC_DT_RNG, scalar2=SC_DT_CTR,
                    op0=ALU.mult, op1=ALU.add))(Sc),
                    waits=S.w("a"), inc="v")
                S.sep("v")
                S.rec("v", (lambda sc: lambda e: e.tensor_tensor(
                    out=z_buf[:, 0:DH * sc].rearrange("p (d t) -> p d t", d=DH),
                    in0=dt_buf[:, 0:DH * sc].rearrange("p (d t) -> p d t", d=DH),
                    in1=x_buf[:, 0:DH * sc].rearrange("p (d t) -> p d t", d=DH),
                    op=ALU.mult))(Sc), inc="v")
                for dd in range(DH):
                    S.rec("v", (lambda dd, sc, x0: lambda e: e.tensor_tensor(
                        out=a_4d[:, :, dd * N:(dd + 1) * N, :],
                        in0=dt_buf[:, 0:DH * sc].rearrange("p (d t) -> p d t", d=DH)
                            [:, dd, x0:x0 + 90].rearrange("p (k j) -> p k j", j=10)
                            .unsqueeze(2).broadcast_to([P, NSEG, N, 10]),
                        in1=A_res[:].rearrange("p (d n) -> p d n", d=DH)[:, dd]
                            .unsqueeze(1).broadcast_to([P, NSEG, N])
                            .unsqueeze(3).broadcast_to([P, NSEG, N, 10]),
                        op=ALU.mult))(dd, Sc, xo),
                        inc="v" if dd == DH - 1 else None)
                v_dta = S.cnt["v"]
                for qq in range(4):
                    lo, hi = qq * 2880, (qq + 1) * 2880
                    S.rec("a", (lambda lo, hi: lambda e: e.activation(
                        out=scr[:, 0:hi - lo], in_=a_buf[:, lo:hi],
                        func=AF.Tanh, scale=SC8_SCALE))(lo, hi),
                        waits=[("v", v_dta)] if qq == 0 else ())
                    S.rec("a", (lambda lo, hi: lambda e: e.activation(
                        out=a_buf[:, lo:hi], in_=scr[:, 0:hi - lo],
                        func=AF.Exp, scale=16.0))(lo, hi),
                        inc="a" if qq == 3 else None)
                a_ready = S.cnt["a"]
                S.rec("v", lambda e: e.tensor_copy(
                    out=ast_buf[:].rearrange("p (k s) -> p k s", k=NSEG),
                    in_=a_4d[:, :, :, 0:1].squeeze(3)),
                    waits=[("a", a_ready)])
                S.sep("v")
                S.rec("v", lambda e: e.memset(a_4d[:, :, :, 0:1], 0.0), inc="v")

                S.rec("a", (lambda sc: lambda e: e.activation(
                    out=bc_sb[0:32, 0:sc], in_=psBC[0:32, 0:sc],
                    func=AF.Copy))(Sc), waits=[("t", t_bc)], inc="a")
                a_bc = S.cnt["a"]
                S.rec("s", (lambda sc: lambda e: e.dma_start(
                    out=bp_row[0:1, 0:N * sc], in_=bc_sb[0:N, 0:sc]))(Sc),
                    waits=[("a", a_bc)], inc="q")
                S.rec("s", (lambda sc: lambda e: e.dma_start(
                    out=c_row[0:1, 0:N * sc], in_=bc_sb[N:32, 0:sc]))(Sc), inc="q")
                q_gather = S.cnt["q"]
                prev_gather_q = q_gather
                tot = N * Sc
                parts = [(i, min(i + 448, tot)) for i in range(0, tot, 448)]
                for rep, row in ((bp_rep, bp_row), (c_rep, c_row)):
                    for lo, hi in parts:
                        S.rec("t", (lambda row, lo, hi: lambda e: nc.tensor.matmul(
                            psREP[:, 0:hi - lo], ones1[:], row[0:1, lo:hi],
                            start=True, stop=True))(row, lo, hi),
                            waits=[("q", q_gather), ("a", S.cnt["a"]),
                                   ("v", prev_Vx1)], inc="t")
                        S.rec("a", (lambda rep, lo, hi: lambda e: e.activation(
                            out=rep[:, lo:hi], in_=psREP[:, 0:hi - lo],
                            func=AF.Copy))(rep, lo, hi),
                            waits=S.w("t"), inc="a")
                a_reps = S.cnt["a"]

                for dd in range(DH):
                    S.rec("v", (lambda dd, sc, x0: lambda e: e.tensor_tensor(
                        out=uh_4d[:, :, dd * N:(dd + 1) * N, :],
                        in0=z_buf[:, 0:DH * sc].rearrange("p (d t) -> p d t", d=DH)
                            [:, dd, x0:x0 + 90].rearrange("p (k j) -> p k j", j=10)
                            .unsqueeze(2).broadcast_to([P, NSEG, N, 10]),
                        in1=bp_rep[:, 0:N * sc].rearrange("p (n t) -> p n t", n=N)
                            [:, :, x0:x0 + 90].rearrange("p n (k j) -> p k n j", j=10),
                        op=ALU.mult))(dd, Sc, xo),
                        waits=[("a", a_reps)] if dd == 0 else (),
                        inc="v" if dd == DH - 1 else None)

                if m == 0:
                    S.rec("v", lambda e: e.tensor_tensor(
                        out=carry[:].rearrange("p (d n) -> p d n", d=DH),
                        in0=z_buf[:, 0:DH * 91].rearrange("p (d t) -> p d t", d=DH)
                            [:, :, 0:1].broadcast_to([P, DH, N]),
                        in1=bp_rep[:, 0:N * 91].rearrange("p (n t) -> p n t", n=N)
                            [:, :, 0:1].transpose([0, 2, 1]).broadcast_to([P, DH, N]),
                        op=ALU.mult), inc="v")
                    s_chain(carry[:])
                    if dbg:
                        S.rec("s", lambda e: e.dma_start(out=dbg_c0, in_=carry[:]),
                              waits=S.w("v"), inc="q")
                        S.rec("s", lambda e: e.dma_start(out=dbg_u, in_=uh_buf[:]), inc="q")
                        S.rec("v", lambda e: e.engine_nop(), waits=S.w("q"), inc="v")
                    S.rec("v", lambda e: e.memset(y_buf[:, 0:DH * 91], 0.0), inc="v")
                    S.rec("v", lambda e: e.tensor_tensor(
                        out=itmp[:].rearrange("p (q d) -> p q d", q=N),
                        in0=carry[:].rearrange("p (d n) -> p d n", d=DH).transpose([0, 2, 1]),
                        in1=c_rep[:, 0:N * 91].rearrange("p (q t) -> p q t", q=N)
                            [:, :, 0:1].broadcast_to([P, N, DH]),
                        op=ALU.mult))
                    S.sep("v")
                    S.sep("v")
                    S.rec("v", lambda e: e.tensor_reduce(
                        out=y_buf[:, 0:DH * 91].rearrange("p (d t) -> p d t", d=DH)[:, :, 0:1].squeeze(2),
                        in_=itmp[:].rearrange("p (q d) -> p q d", q=N).transpose([0, 2, 1]),
                        axis=AX.X, op=ALU.add), inc="v")

                for k in range(NSEG):
                    hprev = carry[:] if k == 0 else uh_4d[:, k - 1, :, 9:10].squeeze(2)
                    S.sep("v")
                    S.rec("v", (lambda k, hp: lambda e: e.tensor_tensor(
                        out=itmp[:], in0=ast_buf[:, k * P:(k + 1) * P],
                        in1=hp, op=ALU.mult))(k, hprev))
                    S.sep("v")
                    S.rec("v", (lambda k: lambda e: e.tensor_tensor(
                        out=uh_4d[:, k, :, 0:1].squeeze(2),
                        in0=uh_4d[:, k, :, 0:1].squeeze(2),
                        in1=itmp[:], op=ALU.add))(k))
                    S.sep("v")
                    S.rec("v", (lambda k: lambda e: e.tensor_tensor_scan(
                        out=uh_buf[:, k * 1280:(k + 1) * 1280],
                        data0=a_buf[:, k * 1280:(k + 1) * 1280],
                        data1=uh_buf[:, k * 1280:(k + 1) * 1280],
                        initial=0.0, op0=ALU.mult, op1=ALU.add))(k), inc="v")
                    s_chain(uh_4d[:, k, :, 9:10].squeeze(2))
                    if dbg and m == 0 and k == 0:
                        S.rec("s", lambda e: e.dma_start(out=dbg_scl[0:1, 0:8], in_=scl[0:1, 0:8]),
                              waits=S.w("v"), inc="q")
                        S.rec("s", lambda e: e.dma_start(out=dbg_scl[:, 8:9], in_=sqs[:, 0:1],
                              allow_slow_non_contiguous=True), inc="q")
                        S.rec("v", lambda e: e.engine_nop(), waits=S.w("q"), inc="v")
                S.sep("v")
                S.rec("v", lambda e: e.tensor_copy(
                    out=carry[:], in_=uh_4d[:, NSEG - 1, :, 9:10].squeeze(2)), inc="v")

                if m == 0 and dbg:
                    S.rec("s", lambda e: e.dma_start(out=dbg_uh, in_=uh_buf[:]),
                          waits=S.w("v"), inc="q")
                    S.rec("s", lambda e: e.dma_start(out=dbg_carry, in_=carry[:]),
                          inc="q")
                if m != 0:
                    S.rec("v", lambda e: e.memset(y_buf[:, 0:DH * 90], 0.0), inc="v")
                def ymul_op(n, sc):
                    wt = w_tmp if n % 2 == 0 else w_tmp2
                    S.rec("v", (lambda n, sc, wt: lambda e: e.tensor_tensor(
                        out=wt[:].rearrange("p (k d j) -> p k d j", k=NSEG, d=DH),
                        in0=uh_4d[:, :, slice(n, P, N), :],
                        in1=c_rep[:, 0:N * sc].rearrange("p (q t) -> p q t", q=N)
                            [:, n, (1 if sc == 91 else 0):(1 if sc == 91 else 0) + 90]
                            .rearrange("p (k j) -> p k j", j=10)
                            .unsqueeze(2).broadcast_to([P, NSEG, DH, 10]),
                        op=ALU.mult))(n, sc, wt))

                def yadd_op(n, sc, x0, last):
                    wt = w_tmp if n % 2 == 0 else w_tmp2
                    S.rec("v", (lambda n, sc, x0, wt: lambda e: e.tensor_tensor(
                        out=y_buf[:, 0:DH * sc].rearrange("p (d t) -> p d t", d=DH)
                            [:, :, x0:x0 + 90].rearrange("p d (k j) -> p k d j", j=10),
                        in0=y_buf[:, 0:DH * sc].rearrange("p (d t) -> p d t", d=DH)
                            [:, :, x0:x0 + 90].rearrange("p d (k j) -> p k d j", j=10),
                        in1=wt[:].rearrange("p (k d j) -> p k d j", k=NSEG, d=DH),
                        op=ALU.add))(n, sc, x0, wt),
                        inc="v" if last else None)

                ymul_op(0, Sc)
                for n in range(1, N):
                    ymul_op(n, Sc)
                    yadd_op(n - 1, Sc, xo, False)
                S.sep("v")
                yadd_op(N - 1, Sc, xo, True)
                if m == 0 and dbg:
                    S.rec("s", lambda e: e.dma_start(out=dbg_y, in_=y_buf[:]),
                          waits=S.w("v"), inc="q")
                    S.rec("s", lambda e: e.dma_start(out=dbg_dt, in_=dt_buf[:]), inc="q")
                    S.rec("s", lambda e: e.dma_start(out=dbg_a, in_=a_buf[:]), inc="q")
                    S.rec("s", lambda e: e.dma_start(out=dbg_ast, in_=ast_buf[:]), inc="q")
                    S.rec("s", lambda e: e.dma_start(out=dbg_bp, in_=bp_rep[:]), inc="q")
                    S.rec("s", lambda e: e.dma_start(out=dbg_cp, in_=c_rep[:]), inc="q")
                    S.rec("v", lambda e: e.engine_nop(), waits=S.w("q"), inc="v")
                S.sep("v")
                for dd in range(DH):
                    S.rec("v", (lambda dd, sc: lambda e: e.scalar_tensor_tensor(
                        out=y_buf[:, dd * sc:(dd + 1) * sc],
                        in0=x_buf[:, dd * sc:(dd + 1) * sc],
                        scalar=dp1_s[:, dd:dd + 1],
                        in1=y_buf[:, dd * sc:(dd + 1) * sc],
                        op0=ALU.mult, op1=ALU.add))(dd, Sc),
                        inc="v" if dd == DH - 1 else None)
                v_u1 = S.cnt["v"]
                S.rec("a", (lambda sc: lambda e: e.activation(
                    out=scr[:, 0:DH * sc], in_=y_buf[:, 0:DH * sc], func=AF.Square))(Sc),
                    waits=[("v", v_u1)], inc="a")
                a_sq = S.cnt["a"]
                for dd in range(DH):
                    S.rec("t", (lambda dd, sc: lambda e: nc.tensor.matmul(
                        psLN[0:1, 0:sc], ones128[:], y_buf[:, dd * sc:(dd + 1) * sc],
                        start=(dd == 0), stop=(dd == DH - 1)))(dd, Sc),
                        waits=[("v", v_u1)] if dd == 0 else (),
                        inc="t" if dd == DH - 1 else None)
                for dd in range(DH):
                    S.rec("t", (lambda dd, sc: lambda e: nc.tensor.matmul(
                        psLN2[0:1, 0:sc], ones128[:], scr[:, dd * sc:(dd + 1) * sc],
                        start=(dd == 0), stop=(dd == DH - 1)))(dd, Sc),
                        waits=[("a", a_sq)] if dd == 0 else (),
                        inc="t" if dd == DH - 1 else None)
                t_ln = S.cnt["t"]
                S.rec("v", (lambda sc: lambda e: e.tensor_scalar(
                    out=lrow[0:1, 0:sc], in0=psLN[0:1, 0:sc],
                    scalar1=1.0 / D, scalar2=None, op0=ALU.mult))(Sc),
                    waits=[("t", t_ln)])
                S.rec("v", (lambda sc: lambda e: e.tensor_scalar(
                    out=lrow[0:1, 91:91 + sc], in0=psLN2[0:1, 0:sc],
                    scalar1=1.0 / D, scalar2=None, op0=ALU.mult))(Sc))
                S.rec("v", (lambda sc: lambda e: e.tensor_tensor(
                    out=lrow[0:1, 182:182 + sc], in0=lrow[0:1, 0:sc],
                    in1=lrow[0:1, 0:sc], op=ALU.mult))(Sc))
                S.sep("v")
                S.rec("v", (lambda sc: lambda e: e.tensor_tensor(
                    out=lrow[0:1, 91:91 + sc], in0=lrow[0:1, 91:91 + sc],
                    in1=lrow[0:1, 182:182 + sc], op=ALU.subtract))(Sc), inc="v")
                S.rec("a", (lambda sc: lambda e: e.activation(
                    out=lrow[0:1, 182:182 + sc], in_=lrow[0:1, 91:91 + sc],
                    func=AF.Ln, bias=cb[0:1, 2:3]))(Sc), waits=S.w("v"))
                S.sep("a")
                S.rec("a", (lambda sc: lambda e: e.activation(
                    out=lrow[0:1, 273:273 + sc], in_=lrow[0:1, 182:182 + sc],
                    func=AF.Exp, scale=-0.5))(Sc), inc="a")
                S.rec("t", (lambda sc: lambda e: nc.tensor.matmul(
                    psREP[:, 0:sc], ones1[:], lrow[0:1, 0:sc],
                    start=True, stop=True))(Sc),
                    waits=S.w("a") + [("v", prev_Vx1)], inc="t")
                S.rec("t", (lambda sc: lambda e: nc.tensor.matmul(
                    psB[:, 0:sc], ones1[:], lrow[0:1, 273:273 + sc],
                    start=True, stop=True))(Sc), inc="t")
                t_rep = S.cnt["t"]
                S.rec("v", (lambda sc: lambda e: e.tensor_tensor(
                    out=x1_buf[:, 0:DH * sc].rearrange("p (d t) -> p d t", d=DH),
                    in0=y_buf[:, 0:DH * sc].rearrange("p (d t) -> p d t", d=DH),
                    in1=psREP[:, 0:sc].unsqueeze(1).broadcast_to([P, DH, sc]),
                    op=ALU.subtract))(Sc),
                    waits=[("t", t_rep), ("q", S.cnt["q"])])
                S.sep("v")
                S.rec("v", (lambda sc: lambda e: e.tensor_tensor(
                    out=x1_buf[:, 0:DH * sc].rearrange("p (d t) -> p d t", d=DH),
                    in0=x1_buf[:, 0:DH * sc].rearrange("p (d t) -> p d t", d=DH),
                    in1=psB[:, 0:sc].unsqueeze(1).broadcast_to([P, DH, sc]),
                    op=ALU.mult))(Sc), inc="v")
                prev_Vx1 = S.cnt["v"]
                S.rec("s", (lambda t0, sc: lambda e: e.dma_start(
                    out=x1_v[:, :, t0:t0 + sc],
                    in_=x1_buf[:, 0:DH * sc].rearrange("p (k c) -> p k c", k=DH)))(tok0, Sc),
                    waits=[("v", prev_Vx1)], inc="q")

            TL = 8191
            S.rec("s", lambda e: e.dma_start(
                out=x_buf[:, 0:DH].rearrange("p (k c) -> p k c", k=DH),
                in_=x_v[:, :, TL:TL + 1], allow_slow_non_contiguous=True),
                waits=[("v", prev_Vx1)], inc="q")
            q_x = S.cnt["q"]
            a_pre_tail = S.cnt["a"]
            for dd in range(DH):
                for k in range(DH):
                    S.rec("t", (lambda dd, k: lambda e: nc.tensor.matmul(
                        psDT[dd % 2][:, 0:1], wdt_v[:, k, dd * P:(dd + 1) * P],
                        x_buf[:, 0:DH].rearrange("p (q c) -> p q c", q=DH)[:, k],
                        start=(k == 0), stop=(k == DH - 1)))(dd, k),
                        waits=[("q", q_x), ("a", a_pre_tail)] if k == 0 else (),
                        inc="t" if k == DH - 1 else None)
                tq = S.cnt["t"]
                S.rec("a", (lambda dd: lambda e: e.activation(
                    out=r_buf[:, dd:dd + 1], in_=psDT[dd % 2][:, 0:1],
                    func=AF.Relu, bias=bdt_s[:, dd:dd + 1]))(dd), waits=[("t", tq)])
                S.rec("a", (lambda dd: lambda e: e.activation(
                    out=va_buf[:, dd:dd + 1], in_=psDT[dd % 2][:, 0:1],
                    func=AF.Abs, bias=bdt_s[:, dd:dd + 1]))(dd), inc="a")
            for k in range(DH):
                S.rec("t", (lambda k: lambda e: nc.tensor.matmul(
                    psBC[0:32, 0:1], wbc_v[:, k, :],
                    x_buf[:, 0:DH].rearrange("p (q c) -> p q c", q=DH)[:, k],
                    start=(k == 0), stop=(k == DH - 1)))(k),
                    waits=[("q", max(prev_gather_q, q_x))] if k == 0 else (),
                    inc="t" if k == DH - 1 else None)
            t_bc = S.cnt["t"]
            S.rec("a", lambda e: e.activation(out=e_buf[:, 0:DH], in_=va_buf[:, 0:DH],
                                              func=AF.Exp, scale=-1.0))
            S.rec("a", lambda e: e.activation(out=va_buf[:, 0:DH], in_=e_buf[:, 0:DH],
                                              func=AF.Ln, bias=cb[:, 1:2]), inc="a")
            S.rec("v", lambda e: e.tensor_tensor(out=r_buf[:, 0:DH], in0=r_buf[:, 0:DH],
                                                 in1=va_buf[:, 0:DH], op=ALU.add),
                  waits=S.w("a"), inc="v")
            S.rec("a", lambda e: e.activation(out=e_buf[:, 0:DH], in_=r_buf[:, 0:DH],
                                              func=AF.Tanh, scale=SC_DT_SCALE,
                                              bias=cb[:, 0:1]),
                  waits=S.w("v"), inc="a")
            S.rec("v", lambda e: e.tensor_scalar(out=dt_buf[:, 0:DH], in0=e_buf[:, 0:DH],
                                                 scalar1=SC_DT_RNG, scalar2=SC_DT_CTR,
                                                 op0=ALU.mult, op1=ALU.add),
                  waits=S.w("a"), inc="v")
            S.sep("v")
            S.rec("v", lambda e: e.tensor_tensor(
                out=a_buf[:, 0:P].rearrange("p (d n) -> p d n", d=DH),
                in0=dt_buf[:, 0:DH].unsqueeze(2).broadcast_to([P, DH, N]),
                in1=A_res[:].rearrange("p (d n) -> p d n", d=DH), op=ALU.mult),
                inc="v")
            S.rec("a", lambda e: e.activation(out=scr[:, 0:P], in_=a_buf[:, 0:P],
                                              func=AF.Tanh, scale=SC8_SCALE),
                  waits=S.w("v"))
            S.sep("a")
            S.rec("a", lambda e: e.activation(out=a_buf[:, 0:P], in_=scr[:, 0:P],
                                              func=AF.Exp, scale=16.0), inc="a")
            S.rec("a", lambda e: e.activation(out=bc_sb[0:32, 0:1], in_=psBC[0:32, 0:1],
                                              func=AF.Copy), waits=[("t", t_bc)], inc="a")
            a_bc = S.cnt["a"]
            S.rec("s", lambda e: e.dma_start(out=bp_row[0:1, 0:N], in_=bc_sb[0:N, 0:1]),
                  waits=[("a", a_bc)], inc="q")
            S.rec("s", lambda e: e.dma_start(out=c_row[0:1, 0:N], in_=bc_sb[N:32, 0:1]),
                  inc="q")
            q_gather = S.cnt["q"]
            S.rec("t", lambda e: nc.tensor.matmul(psREP[:, 0:N], ones1[:],
                                                  bp_row[0:1, 0:N], start=True, stop=True),
                  waits=[("q", q_gather), ("a", S.cnt["a"]), ("v", S.cnt["v"])], inc="t")
            S.rec("a", lambda e: e.activation(out=bp_rep[:, 0:N], in_=psREP[:, 0:N],
                                              func=AF.Copy), waits=S.w("t"), inc="a")
            S.rec("t", lambda e: nc.tensor.matmul(psREP[:, 0:N], ones1[:],
                                                  c_row[0:1, 0:N], start=True, stop=True),
                  waits=S.w("a"), inc="t")
            S.rec("a", lambda e: e.activation(out=c_rep[:, 0:N], in_=psREP[:, 0:N],
                                              func=AF.Copy), waits=S.w("t"), inc="a")
            S.rec("v", lambda e: e.tensor_tensor(out=itmp[:, 0:DH], in0=dt_buf[:, 0:DH],
                                                 in1=x_buf[:, 0:DH], op=ALU.mult),
                  waits=S.w("a"))
            S.sep("v")
            S.sep("v")
            S.sep("v")
            S.rec("v", lambda e: e.tensor_tensor(
                out=uh_buf[:, 0:P].rearrange("p (d n) -> p d n", d=DH),
                in0=itmp[:, 0:DH].unsqueeze(2).broadcast_to([P, DH, N]),
                in1=bp_rep[:, 0:N].unsqueeze(1).broadcast_to([P, DH, N]),
                op=ALU.mult))
            S.sep("v")
            S.sep("v")
            S.rec("v", lambda e: e.tensor_tensor(out=itmp[:], in0=a_buf[:, 0:P],
                                                 in1=carry[:], op=ALU.mult))
            S.sep("v")
            S.sep("v")
            S.rec("v", lambda e: e.tensor_tensor(out=uh_buf[:, 0:P], in0=uh_buf[:, 0:P],
                                                 in1=itmp[:], op=ALU.add), inc="v")
            S.rec("v", lambda e: e.memset(y_buf[:, 0:DH], 0.0), inc="v")
            S.sep("v")
            S.rec("v", lambda e: e.tensor_tensor(
                out=itmp[:].rearrange("p (q d) -> p q d", q=N),
                in0=uh_buf[:, 0:P].rearrange("p (d n) -> p d n", d=DH).transpose([0, 2, 1]),
                in1=c_rep[:, 0:N].unsqueeze(2).broadcast_to([P, N, DH]),
                op=ALU.mult))
            S.sep("v")
            S.sep("v")
            S.rec("v", lambda e: e.tensor_reduce(
                out=y_buf[:, 0:DH],
                in_=itmp[:].rearrange("p (q d) -> p q d", q=N).transpose([0, 2, 1]),
                axis=AX.X, op=ALU.add), inc="v")
            S.sep("v")
            for dd in range(DH):
                S.rec("v", (lambda dd: lambda e: e.scalar_tensor_tensor(
                    out=y_buf[:, dd:dd + 1], in0=x_buf[:, dd:dd + 1],
                    scalar=dp1_s[:, dd:dd + 1], in1=y_buf[:, dd:dd + 1],
                    op0=ALU.mult, op1=ALU.add))(dd),
                    inc="v" if dd == DH - 1 else None)
            v_u1 = S.cnt["v"]
            S.rec("a", lambda e: e.activation(out=scr[:, 0:DH], in_=y_buf[:, 0:DH],
                                              func=AF.Square),
                  waits=[("v", v_u1)], inc="a")
            a_sq = S.cnt["a"]
            for dd in range(DH):
                S.rec("t", (lambda dd: lambda e: nc.tensor.matmul(
                    psLN[0:1, 0:1], ones128[:], y_buf[:, dd:dd + 1],
                    start=(dd == 0), stop=(dd == DH - 1)))(dd),
                    waits=[("v", v_u1)] if dd == 0 else (),
                    inc="t" if dd == DH - 1 else None)
            for dd in range(DH):
                S.rec("t", (lambda dd: lambda e: nc.tensor.matmul(
                    psLN2[0:1, 0:1], ones128[:], scr[:, dd:dd + 1],
                    start=(dd == 0), stop=(dd == DH - 1)))(dd),
                    waits=[("a", a_sq)] if dd == 0 else (),
                    inc="t" if dd == DH - 1 else None)
            t_ln = S.cnt["t"]
            S.rec("v", lambda e: e.tensor_scalar(out=lrow[0:1, 0:1], in0=psLN[0:1, 0:1],
                                                 scalar1=1.0 / D, scalar2=None,
                                                 op0=ALU.mult), waits=[("t", t_ln)])
            S.rec("v", lambda e: e.tensor_scalar(out=lrow[0:1, 1:2], in0=psLN2[0:1, 0:1],
                                                 scalar1=1.0 / D, scalar2=None,
                                                 op0=ALU.mult))
            S.rec("v", lambda e: e.tensor_tensor(out=lrow[0:1, 2:3], in0=lrow[0:1, 0:1],
                                                 in1=lrow[0:1, 0:1], op=ALU.mult))
            S.sep("v")
            S.sep("v")
            S.rec("v", lambda e: e.tensor_tensor(out=lrow[0:1, 1:2], in0=lrow[0:1, 1:2],
                                                 in1=lrow[0:1, 2:3], op=ALU.subtract),
                  inc="v")
            S.rec("a", lambda e: e.activation(out=lrow[0:1, 2:3], in_=lrow[0:1, 1:2],
                                              func=AF.Ln, bias=cb[0:1, 2:3]), waits=S.w("v"))
            S.sep("a")
            S.rec("a", lambda e: e.activation(out=lrow[0:1, 3:4], in_=lrow[0:1, 2:3],
                                              func=AF.Exp, scale=-0.5), inc="a")
            S.rec("t", lambda e: nc.tensor.matmul(psREP[:, 0:1], ones1[:], lrow[0:1, 0:1],
                                                  start=True, stop=True),
                  waits=S.w("a") + [("v", S.cnt["v"])], inc="t")
            S.rec("t", lambda e: nc.tensor.matmul(psB[:, 0:1], ones1[:], lrow[0:1, 3:4],
                                                  start=True, stop=True), inc="t")
            t_rep = S.cnt["t"]
            S.rec("v", lambda e: e.tensor_tensor(
                out=x1_buf[:, 0:DH], in0=y_buf[:, 0:DH],
                in1=psREP[:, 0:1].broadcast_to([P, DH]), op=ALU.subtract),
                waits=[("t", t_rep), ("q", S.cnt["q"])])
            S.sep("v")
            S.rec("v", lambda e: e.tensor_tensor(
                out=x1_buf[:, 0:DH], in0=x1_buf[:, 0:DH],
                in1=psB[:, 0:1].broadcast_to([P, DH]), op=ALU.mult), inc="v")
            v_x1t = S.cnt["v"]
            S.rec("s", lambda e: e.dma_start(
                out=x1_v[:, :, TL:TL + 1],
                in_=x1_buf[:, 0:DH].rearrange("p (k c) -> p k c", k=DH),
                allow_slow_non_contiguous=True),
                waits=[("v", v_x1t)], inc="q")
            q_ph1_done = S.cnt["q"]

        with contextlib.ExitStack() as ph2:
            w1q_s = sb("w1q_s", [P, DH * HID], BF16, ph2)
            w2q_b = [sb("w2q_b0", [P, HH * P], BF16, ph2),
                     sb("w2q_b1", [P, HH * P], BF16, ph2)]
            h1s = sb("h1s", [P, HH * T2], F32, ph2)
            h1q_v = h1s.ap().bitcast(BF16)[:, 0:HH * T2]
            x1c = sb("x1c", [P, DH * (T2 + 2)], F32, ph2)
            x1q = sb("x1q", [P, DH * (T2 + 2)], BF16, ph2)
            u2b = sb("u2b", [P, DH * T2], F32, ph2)
            qrA = sb("qrA", [1, T2 + 2], F32, ph2)
            qrB = sb("qrB", [1, T2 + 2], F32, ph2)
            qrC = sb("qrC", [1, T2 + 2], F32, ph2)
            qrD = sb("qrD", [1, T2 + 2], F32, ph2)
            ab_scr = sb("ab_scr", [P, T2 + 2], F32, ph2)
            cvt = sb("cvt", [P, T2 + 2], F32, ph2)
            scr2 = sb("scr2", [P, DH * T2], F32, ph2)
            rows = sb("rows", [1, 2048], F32, ph2)
            fsb = sb("fsb", [P, T2 + 2], F32, ph2)
            psA = pt("psA", [P, 512], ph2)
            psBk = pt("psBk", [P, 512], ph2)
            psC = pt("psC", [P, 512], ph2)
            psD = pt("psD", [1, 512], ph2)
            psE = pt("psE", [1, 512], ph2)
            mxA = scr2.ap()[:, 452:452 + T2 + 2]
            mxB = scr2.ap()[:, 904:904 + T2 + 2]
            g3 = u2b.ap()[0:1, 0:3 * (T2 + 2)]

            x1c_v = x1c.ap().rearrange("p (k c2) -> p k c2", k=DH)
            x1q_3 = x1q.ap().rearrange("p (k c2) -> p k c2", k=DH)
            u2_3 = u2b.ap().rearrange("p (k c2) -> p k c2", k=DH)
            scr2_3 = scr2.ap().rearrange("p (k c2) -> p k c2", k=DH)
            w1v = w1q_s.ap().rearrange("p (k h) -> p k h", k=DH)
            w2sc_v = w2qsc.rearrange("p (d k c2) -> p d k c2", d=DH, k=HH)

            S.rec("s", lambda e: e.dma_start(out=w1q_s[:], in_=w1qsc),
                  waits=S.barrier_w() + [("q", q_wq_done)], inc="q")
            q_w1 = S.cnt["q"]
            if dbg:
                S.rec("s", lambda e: e.dma_start(out=dbgf_w1q, in_=w1q_s[:]),
                      waits=[("q", q_w1)], inc="q")

            prev_store_v = S.cnt["v"]
            for c in range(NCH2):
                o0 = c * T2
                O = min(T2, T - o0)
                padL = (c == 0)
                padR = (o0 + O == T)
                lo = max(0, o0 - 1)
                W = min(T, o0 + O + 1) - lo
                cs = 1 if padL else 0
                WB = O + 2

                S.rec("s", (lambda lo, W, cs: lambda e: e.dma_start(
                    out=x1c_v[:, :, cs:cs + W],
                    in_=x1_v[:, :, lo:lo + W]))(lo, W, cs),
                    waits=[("v", prev_store_v), ("q", q_ph1_done)], inc="q")
                q_x1 = S.cnt["q"]
                if padL:
                    S.rec("v", lambda e: e.memset(x1c_v[:, :, 0:1], 0.0))
                if padR:
                    S.rec("v", (lambda WB: lambda e: e.memset(
                        x1c_v[:, :, WB - 1:WB], 0.0))(WB))
                S.rec("v", lambda e: e.engine_nop(), waits=[("q", q_x1)], inc="v")
                v_x1r = S.cnt["v"]

                S.rec("v", (lambda WB: lambda e: e.scalar_tensor_tensor(
                    out=mxB[:, 0:WB], in0=x1c_v[:, 0, 0:WB], scalar=-1.0,
                    in1=x1c_v[:, 0, 0:WB], op0=ALU.mult, op1=ALU.max))(WB),
                    waits=[("v", v_x1r)])
                for dd in range(1, DH):
                    S.sep("v")
                    S.rec("v", (lambda dd, WB: lambda e: e.scalar_tensor_tensor(
                        out=mxA[:, 0:WB], in0=x1c_v[:, dd, 0:WB], scalar=-1.0,
                        in1=mxB[:, 0:WB], op0=ALU.mult, op1=ALU.max))(dd, WB))
                    S.sep("v")
                    S.rec("v", (lambda dd, WB: lambda e: e.scalar_tensor_tensor(
                        out=mxB[:, 0:WB], in0=x1c_v[:, dd, 0:WB], scalar=1.0,
                        in1=mxA[:, 0:WB], op0=ALU.mult, op1=ALU.max))(dd, WB))
                cur, nxt = mxB, mxA
                for k in (16, 8, 4, 2, 1):
                    mask = [(i + k) % 32 for i in range(32)]
                    S.sep("v")
                    S.rec("v", (lambda cur, WB, mask: lambda e: e.stream_shuffle(
                        out=scr2[:, 0:WB], in_=cur[:, 0:WB], mask=mask))(cur, WB, mask))
                    S.sep("v")
                    S.rec("v", (lambda cur, nxt, WB: lambda e: e.tensor_tensor(
                        out=nxt[:, 0:WB], in0=cur[:, 0:WB], in1=scr2[:, 0:WB],
                        op=ALU.max))(cur, nxt, WB),
                        inc="v" if k == 1 else None)
                v_tree1 = S.cnt["v"]
                for j, pp in enumerate((32, 64, 96)):
                    S.rec("s", (lambda j, pp, WB: lambda e: e.dma_start(
                        out=g3[0:1, j * (T2 + 2):j * (T2 + 2) + WB],
                        in_=mxA[pp:pp + 1, 0:WB]))(j, pp, WB),
                        waits=[("v", v_tree1)] if j == 0 else (), inc="q")
                q_g1 = S.cnt["q"]
                S.rec("v", (lambda WB: lambda e: e.tensor_tensor(
                    out=qrB[0:1, 0:WB], in0=mxA[0:1, 0:WB], in1=g3[0:1, 0:WB],
                    op=ALU.max))(WB), waits=[("q", q_g1)])
                S.sep("v")
                S.rec("v", (lambda WB: lambda e: e.tensor_tensor(
                    out=qrD[0:1, 0:WB], in0=qrB[0:1, 0:WB],
                    in1=g3[0:1, T2 + 2:T2 + 2 + WB], op=ALU.max))(WB))
                S.sep("v")
                S.rec("v", (lambda WB: lambda e: e.tensor_tensor(
                    out=qrA[0:1, 0:WB], in0=qrD[0:1, 0:WB],
                    in1=g3[0:1, 2 * (T2 + 2):2 * (T2 + 2) + WB], op=ALU.max))(WB),
                    inc="v")
                S.sep("v")
                S.rec("v", (lambda WB: lambda e: e.tensor_scalar(
                    out=rows[0:1, 0:WB], in0=qrA[0:1, 0:WB], scalar1=1e-5,
                    scalar2=None, op0=ALU.max))(WB))
                S.sep("v")
                S.rec("v", (lambda WB: lambda e: e.reciprocal(
                    out=rows[0:1, 512:512 + WB], in_=rows[0:1, 0:WB]))(WB))
                S.sep("v")
                S.rec("v", (lambda WB: lambda e: e.tensor_scalar(
                    out=rows[0:1, 512:512 + WB], in0=rows[0:1, 512:512 + WB],
                    scalar1=127.0, scalar2=None, op0=ALU.mult))(WB))
                S.rec("v", (lambda WB: lambda e: e.tensor_scalar(
                    out=rows[0:1, 1024:1024 + WB], in0=rows[0:1, 0:WB],
                    scalar1=consts[0:1, 0:1], scalar2=None, op0=ALU.mult))(WB),
                    inc="v")
                v_rows = S.cnt["v"]
                S.rec("t", (lambda WB: lambda e: nc.tensor.matmul(
                    psA[:, 0:WB], ones1[:], rows[0:1, 512:512 + WB],
                    start=True, stop=True))(WB), waits=[("v", v_rows)], inc="t")
                S.rec("t", (lambda WB: lambda e: nc.tensor.matmul(
                    psBk[:, 0:WB], ones1[:], rows[0:1, 1024:1024 + WB],
                    start=True, stop=True))(WB), inc="t")
                t_reps = S.cnt["t"]
                S.rec("v", (lambda WB: lambda e: e.tensor_scalar(
                    out=fsb[:, 0:WB], in0=psBk[:, 0:WB], scalar1=1.0, scalar2=None,
                    op0=ALU.mult))(WB), waits=[("t", t_reps)], inc="v")
                def q1mul(dd, WB):
                    qb = cvt if dd % 2 == 0 else ab_scr
                    S.rec("v", (lambda dd, WB, qb: lambda e: e.tensor_tensor(
                        out=qb[:, 0:WB], in0=x1c_v[:, dd, 0:WB],
                        in1=psA[:, 0:WB], op=ALU.mult))(dd, WB, qb))

                def q1mag(dd, WB, last):
                    qb = cvt if dd % 2 == 0 else ab_scr
                    S.rec("v", (lambda dd, WB, qb: lambda e: e.tensor_scalar(
                        out=x1q_3[:, dd, 0:WB], in0=qb[:, 0:WB],
                        scalar1=MAGIC, scalar2=MAGIC,
                        op0=ALU.add, op1=ALU.subtract))(dd, WB, qb),
                        inc="v" if last else None)

                q1mul(0, WB)
                for dd in range(1, DH):
                    q1mul(dd, WB)
                    q1mag(dd - 1, WB, False)
                S.sep("v")
                q1mag(DH - 1, WB, True)
                v_x1q = S.cnt["v"]

                a_silu = {}
                for h in range(HH):
                    pb = psC if h % 2 == 0 else psA
                    for k in range(DH):
                        S.rec("t", (lambda h, k, pb, WB: lambda e: nc.tensor.matmul(
                            pb[:, 0:WB], w1v[:, k, h * P:(h + 1) * P],
                            x1q_3[:, k, 0:WB],
                            start=(k == 0), stop=(k == DH - 1)))(h, k, pb, WB),
                            waits=([("v", v_x1q), ("q", q_w1)] if h <= 1
                                   else [("v", S.cnt["v"])]) if k == 0 else (),
                            inc="t" if k == DH - 1 else None)
                    t_h = S.cnt["t"]
                    S.rec("v", (lambda pb, WB: lambda e: e.tensor_tensor(
                        out=scr2[:, 0:WB], in0=pb[:, 0:WB], in1=fsb[:, 0:WB],
                        op=ALU.mult))(pb, WB),
                        waits=[("t", t_h), ("a", S.cnt["a"])])
                    S.sep("v")
                    S.rec("v", (lambda h, O: lambda e: e.tensor_scalar(
                        out=cvt[:, 0:O], in0=scr2[:, 1:1 + O],
                        scalar1=convw_s[:, h * 3 + 1:h * 3 + 2], scalar2=None,
                        op0=ALU.mult))(h, O))
                    S.sep("v")
                    S.rec("v", (lambda h, O: lambda e: e.scalar_tensor_tensor(
                        out=cvt[:, 0:O], in0=scr2[:, 0:O],
                        scalar=convw_s[:, h * 3:h * 3 + 1], in1=cvt[:, 0:O],
                        op0=ALU.mult, op1=ALU.add))(h, O))
                    S.sep("v")
                    S.rec("v", (lambda h, O: lambda e: e.scalar_tensor_tensor(
                        out=cvt[:, 0:O], in0=scr2[:, 2:2 + O],
                        scalar=convw_s[:, h * 3 + 2:h * 3 + 3], in1=cvt[:, 0:O],
                        op0=ALU.mult, op1=ALU.add))(h, O), inc="v")
                    v_cv = S.cnt["v"]
                    S.rec("a", (lambda h, O: lambda e: e.activation(
                        out=h1s[:, h * T2:h * T2 + O], in_=cvt[:, 0:O],
                        func=AF.Silu, bias=convb_s[:, h:h + 1]))(h, O),
                        waits=[("v", v_cv)], inc="a")
                    a_silu[h] = S.cnt["a"]
                    if h >= 1:
                        hh0 = h - 1
                        if hh0 == 0:
                            S.rec("v", (lambda O: lambda e: e.scalar_tensor_tensor(
                                out=mxB[:, 0:O], in0=h1s[:, 0:O], scalar=-1.0,
                                in1=h1s[:, 0:O], op0=ALU.mult, op1=ALU.max))(O),
                                waits=[("a", a_silu[0])])
                        else:
                            S.sep("v")
                            S.rec("v", (lambda hh0, O: lambda e: e.scalar_tensor_tensor(
                                out=mxA[:, 0:O], in0=h1s[:, hh0 * T2:hh0 * T2 + O],
                                scalar=-1.0, in1=mxB[:, 0:O],
                                op0=ALU.mult, op1=ALU.max))(hh0, O),
                                waits=[("a", a_silu[hh0])])
                            S.sep("v")
                            S.rec("v", (lambda hh0, O: lambda e: e.scalar_tensor_tensor(
                                out=mxB[:, 0:O], in0=h1s[:, hh0 * T2:hh0 * T2 + O],
                                scalar=1.0, in1=mxA[:, 0:O],
                                op0=ALU.mult, op1=ALU.max))(hh0, O))
                S.rec("a", lambda e: e.activation(
                    out=cvt[:, 0:2], in_=h1s[:, (HH - 1) * T2:(HH - 1) * T2 + 2],
                    func=AF.Silu, bias=convb_s[:, HH - 1:HH]), inc="a")
                a_conf = S.cnt["a"]
                S.sep("v")
                S.rec("v", (lambda O: lambda e: e.scalar_tensor_tensor(
                    out=mxA[:, 0:O], in0=h1s[:, (HH - 1) * T2:(HH - 1) * T2 + O],
                    scalar=-1.0, in1=mxB[:, 0:O], op0=ALU.mult, op1=ALU.max))(O),
                    waits=[("a", a_conf)])
                S.sep("v")
                S.rec("v", (lambda O: lambda e: e.scalar_tensor_tensor(
                    out=mxB[:, 0:O], in0=h1s[:, (HH - 1) * T2:(HH - 1) * T2 + O],
                    scalar=1.0, in1=mxA[:, 0:O], op0=ALU.mult, op1=ALU.max))(O))
                cur, nxt = mxB, mxA
                for k in (16, 8, 4, 2, 1):
                    mask = [(i + k) % 32 for i in range(32)]
                    S.sep("v")
                    S.rec("v", (lambda cur, O, mask: lambda e: e.stream_shuffle(
                        out=scr2[:, 0:O], in_=cur[:, 0:O], mask=mask))(cur, O, mask))
                    S.sep("v")
                    S.rec("v", (lambda cur, nxt, O: lambda e: e.tensor_tensor(
                        out=nxt[:, 0:O], in0=cur[:, 0:O], in1=scr2[:, 0:O],
                        op=ALU.max))(cur, nxt, O),
                        inc="v" if k == 1 else None)
                v_tree2 = S.cnt["v"]
                for j, pp in enumerate((32, 64, 96)):
                    S.rec("s", (lambda j, pp, O: lambda e: e.dma_start(
                        out=g3[0:1, j * (T2 + 2):j * (T2 + 2) + O],
                        in_=mxA[pp:pp + 1, 0:O]))(j, pp, O),
                        waits=[("v", v_tree2)] if j == 0 else (), inc="q")
                q_g2 = S.cnt["q"]
                S.rec("v", (lambda O: lambda e: e.tensor_tensor(
                    out=qrB[0:1, 0:O], in0=mxA[0:1, 0:O], in1=g3[0:1, 0:O],
                    op=ALU.max))(O), waits=[("q", q_g2)])
                S.sep("v")
                S.rec("v", (lambda O: lambda e: e.tensor_tensor(
                    out=qrD[0:1, 0:O], in0=qrB[0:1, 0:O],
                    in1=g3[0:1, T2 + 2:T2 + 2 + O], op=ALU.max))(O))
                S.sep("v")
                S.rec("v", (lambda O: lambda e: e.tensor_tensor(
                    out=qrC[0:1, 0:O], in0=qrD[0:1, 0:O],
                    in1=g3[0:1, 2 * (T2 + 2):2 * (T2 + 2) + O], op=ALU.max))(O),
                    inc="v")
                if dbg and c == 0:
                    S.rec("s", lambda e: e.dma_start(out=dbgf_h1s, in_=h1s[:]),
                          waits=S.w("v", "a"), inc="q")
                    S.rec("v", lambda e: e.engine_nop(), waits=S.w("q"), inc="v")
                S.sep("v")
                S.rec("v", (lambda O: lambda e: e.tensor_scalar(
                    out=rows[0:1, 0:O], in0=qrC[0:1, 0:O], scalar1=1e-5,
                    scalar2=None, op0=ALU.max))(O))
                S.sep("v")
                S.rec("v", (lambda O: lambda e: e.reciprocal(
                    out=rows[0:1, 512:512 + O], in_=rows[0:1, 0:O]))(O))
                S.sep("v")
                S.rec("v", (lambda O: lambda e: e.tensor_scalar(
                    out=rows[0:1, 512:512 + O], in0=rows[0:1, 512:512 + O],
                    scalar1=127.0, scalar2=None, op0=ALU.mult))(O))
                S.rec("v", (lambda O: lambda e: e.tensor_scalar(
                    out=rows[0:1, 1024:1024 + O], in0=rows[0:1, 0:O],
                    scalar1=consts[0:1, 1:2], scalar2=None, op0=ALU.mult))(O),
                    inc="v")
                v_r2 = S.cnt["v"]
                S.rec("t", (lambda O: lambda e: nc.tensor.matmul(
                    psA[:, 0:O], ones1[:], rows[0:1, 512:512 + O],
                    start=True, stop=True))(O), waits=[("v", v_r2)], inc="t")
                S.rec("t", (lambda O: lambda e: nc.tensor.matmul(
                    psBk[:, 0:O], ones1[:], rows[0:1, 1024:1024 + O],
                    start=True, stop=True))(O), inc="t")
                t_r2 = S.cnt["t"]
                S.rec("v", (lambda O: lambda e: e.tensor_scalar(
                    out=fsb[:, 0:O], in0=psBk[:, 0:O], scalar1=1.0, scalar2=None,
                    op0=ALU.mult))(O), waits=[("t", t_r2)], inc="v")
                def q2mul(h, O):
                    qb = cvt if h % 2 == 0 else ab_scr
                    S.rec("v", (lambda h, O, qb: lambda e: e.tensor_tensor(
                        out=qb[:, 0:O], in0=h1s[:, h * T2:h * T2 + O],
                        in1=psA[:, 0:O], op=ALU.mult))(h, O, qb))

                def q2mag(h, O, last):
                    qb = cvt if h % 2 == 0 else ab_scr
                    S.rec("v", (lambda h, O, qb: lambda e: e.tensor_scalar(
                        out=h1q_v[:, h * T2:h * T2 + O], in0=qb[:, 0:O],
                        scalar1=MAGIC, scalar2=MAGIC,
                        op0=ALU.add, op1=ALU.subtract))(h, O, qb),
                        inc="v" if last else None)

                q2mul(0, O)
                for h in range(1, HH):
                    q2mul(h, O)
                    q2mag(h - 1, O, False)
                S.sep("v")
                q2mag(HH - 1, O, True)
                v_h1q = S.cnt["v"]
                if dbg and c == 0:
                    S.rec("s", lambda e: e.dma_start(out=dbgf_h1q, in_=h1q_v),
                          waits=[("v", v_h1q)], inc="q")
                    S.rec("v", lambda e: e.engine_nop(), waits=S.w("q"), inc="v")

                for dd in range(DH):
                    S.rec("s", (lambda dd: lambda e: e.dma_start(
                        out=w2q_b[dd % 2][:], in_=w2sc_v[:, dd]))(dd),
                        waits=[("t", S.cnt["t"]), ("q", q_wq_done)], inc="q")
                    q_w2 = S.cnt["q"]
                    for k in range(HH):
                        S.rec("t", (lambda dd, k, O: lambda e: nc.tensor.matmul(
                            psC[:, 0:O], w2q_b[dd % 2][:, k * P:(k + 1) * P],
                            h1q_v[:, k * T2:k * T2 + O],
                            start=(k == 0), stop=(k == HH - 1)))(dd, k, O),
                            waits=[("q", q_w2), ("v", v_h1q)] if k == 0 else (),
                            inc="t" if k == HH - 1 else None)
                    t_m2 = S.cnt["t"]
                    S.rec("v", (lambda dd, O: lambda e: e.tensor_tensor(
                        out=u2_3[:, dd, 0:O], in0=psC[:, 0:O], in1=fsb[:, 0:O],
                        op=ALU.mult))(dd, O),
                        waits=[("t", t_m2)] + ([("q", S.cnt["q"])] if dd == 0 else []))
                    S.sep("v")
                    S.rec("v", (lambda dd, O: lambda e: e.tensor_tensor(
                        out=u2_3[:, dd, 0:O], in0=u2_3[:, dd, 0:O],
                        in1=x1c_v[:, dd, 1:1 + O], op=ALU.add))(dd, O), inc="v")
                v_u2 = S.cnt["v"]
                S.rec("a", (lambda O: lambda e: e.activation(
                    out=scr2_3[:, :, 0:O], in_=u2_3[:, :, 0:O], func=AF.Square))(O),
                    waits=[("v", v_u2)], inc="a")
                a_sq2 = S.cnt["a"]
                for dd in range(DH):
                    S.rec("t", (lambda dd, O: lambda e: nc.tensor.matmul(
                        psD[0:1, 0:O], ones128[:], u2_3[:, dd, 0:O],
                        start=(dd == 0), stop=(dd == DH - 1)))(dd, O),
                        waits=[("v", v_u2)] if dd == 0 else (),
                        inc="t" if dd == DH - 1 else None)
                for dd in range(DH):
                    S.rec("t", (lambda dd, O: lambda e: nc.tensor.matmul(
                        psE[0:1, 0:O], ones128[:], scr2_3[:, dd, 0:O],
                        start=(dd == 0), stop=(dd == DH - 1)))(dd, O),
                        waits=[("a", a_sq2)] if dd == 0 else (),
                        inc="t" if dd == DH - 1 else None)
                t_ln2 = S.cnt["t"]
                S.rec("v", (lambda O: lambda e: e.tensor_scalar(
                    out=rows[0:1, 0:O], in0=psD[0:1, 0:O], scalar1=1.0 / D,
                    scalar2=None, op0=ALU.mult))(O), waits=[("t", t_ln2)])
                S.rec("v", (lambda O: lambda e: e.tensor_scalar(
                    out=rows[0:1, 512:512 + O], in0=psE[0:1, 0:O], scalar1=1.0 / D,
                    scalar2=None, op0=ALU.mult))(O))
                S.rec("v", (lambda O: lambda e: e.tensor_tensor(
                    out=rows[0:1, 1024:1024 + O], in0=rows[0:1, 0:O],
                    in1=rows[0:1, 0:O], op=ALU.mult))(O))
                S.sep("v")
                S.rec("v", (lambda O: lambda e: e.tensor_tensor(
                    out=rows[0:1, 512:512 + O], in0=rows[0:1, 512:512 + O],
                    in1=rows[0:1, 1024:1024 + O], op=ALU.subtract))(O), inc="v")
                S.rec("a", (lambda O: lambda e: e.activation(
                    out=rows[0:1, 1024:1024 + O], in_=rows[0:1, 512:512 + O],
                    func=AF.Ln, bias=cb[0:1, 2:3]))(O), waits=S.w("v"))
                S.sep("a")
                S.rec("a", (lambda O: lambda e: e.activation(
                    out=rows[0:1, 1536:1536 + O], in_=rows[0:1, 1024:1024 + O],
                    func=AF.Exp, scale=-0.5))(O), inc="a")
                S.rec("t", (lambda O: lambda e: nc.tensor.matmul(
                    psA[:, 0:O], ones1[:], rows[0:1, 0:O],
                    start=True, stop=True))(O), waits=S.w("a"), inc="t")
                S.rec("t", (lambda O: lambda e: nc.tensor.matmul(
                    psBk[:, 0:O], ones1[:], rows[0:1, 1536:1536 + O],
                    start=True, stop=True))(O), inc="t")
                t_rp2 = S.cnt["t"]
                S.rec("v", (lambda O: lambda e: e.tensor_tensor(
                    out=u2_3[:, :, 0:O], in0=u2_3[:, :, 0:O],
                    in1=psA[:, 0:O].unsqueeze(1).broadcast_to([P, DH, O]),
                    op=ALU.subtract))(O), waits=[("t", t_rp2)])
                S.sep("v")
                S.rec("v", (lambda O: lambda e: e.tensor_tensor(
                    out=u2_3[:, :, 0:O], in0=u2_3[:, :, 0:O],
                    in1=psBk[:, 0:O].unsqueeze(1).broadcast_to([P, DH, O]),
                    op=ALU.mult))(O), inc="v")
                prev_store_v = S.cnt["v"]
                S.rec("s", (lambda o0, O: lambda e: e.dma_start(
                    out=out_v[:, :, o0:o0 + O],
                    in_=u2_3[:, :, 0:O]))(o0, O),
                    waits=[("v", prev_store_v)], inc="q")

            q_final = S.cnt["q"]

            with nc.Block() as block:
                @block.sync
                def _(e):
                    S.emit("s", e, sems)
                    e.wait_ge(sems["q"], q_final)

                @block.vector
                def _(e):
                    S.emit("v", e, sems)
                    e.wait_ge(sems["q"], q_final)

                @block.scalar
                def _(e):
                    S.emit("a", e, sems)

                @block.tensor
                def _(e):
                    S.emit("t", e, sems)

                @block.gpsimd
                def _(e):
                    S.emit("p", e, sems)

    return nc


_NC_CACHE = None


def kernel(L0_attn_x, L0_attn_W_dt, L0_attn_b_dt, L0_attn_W_B, L0_attn_W_C,
           L0_attn_A_log, L0_attn_D_param, L0_ffn_w1, L0_ffn_conv_w,
           L0_ffn_conv_b, L0_ffn_w2):
    global _NC_CACHE
    B = np.asarray(L0_attn_x).shape[0]
    assert B == 8

    def fm(a, cols):
        a = np.asarray(a)
        R = a.shape[0]
        return np.ascontiguousarray(
            a.reshape(R // P, P, cols).transpose(1, 0, 2).reshape(P, (R // P) * cols)
        ).astype(np.float32)

    wdtT_h = fm(np.asarray(L0_attn_W_dt).T, D)
    wbc = np.concatenate([np.asarray(L0_attn_W_B), np.asarray(L0_attn_W_C)], axis=0)
    wbcT_h = fm(np.ascontiguousarray(wbc.T), 32)
    alog_h = fm(np.asarray(L0_attn_A_log), N)
    bdt_h = np.ascontiguousarray(np.asarray(L0_attn_b_dt).reshape(DH, P).T).astype(np.float32)
    dpar_h = np.ascontiguousarray(np.asarray(L0_attn_D_param).reshape(DH, P).T).astype(np.float32)
    convw_h = fm(np.asarray(L0_ffn_conv_w)[:, 0, :], 3)
    convb_h = np.ascontiguousarray(np.asarray(L0_ffn_conv_b).reshape(HH, P).T).astype(np.float32)
    w1T_h = fm(np.ascontiguousarray(np.asarray(L0_ffn_w1).T), HID)
    w2T_h = np.ascontiguousarray(
        np.asarray(L0_ffn_w2).reshape(DH, P, HH, P).transpose(3, 0, 2, 1)
        .reshape(P, DH * HID)).astype(np.float32)
    idn_h = np.eye(P, dtype=np.float32)

    if _NC_CACHE is None:
        _NC_CACHE = build_program()
    nc = _NC_CACHE

    in_maps = []
    for b in range(B):
        xt = np.ascontiguousarray(np.asarray(L0_attn_x[b]).T).astype(np.float32)
        in_maps.append(dict(x_t=xt, wdtT=wdtT_h, wbcT=wbcT_h, alog=alog_h,
                            bdt=bdt_h, dpar=dpar_h, convw=convw_h, convb=convb_h,
                            idn=idn_h, w1T=w1T_h, w2T=w2T_h))
    res = run_bass_kernel_spmd(nc, in_maps, core_ids=list(range(B)))
    out = np.empty((B, T, D), dtype=np.float32)
    for b in range(B):
        out[b] = np.asarray(res.results[b]["out_t"]).T
    return out
